# version 01
# baseline (speedup 1.0000x reference)
import contextlib
import numpy as np
import concourse.bass as bass
import concourse.mybir as mybir
from concourse.bass_utils import run_bass_kernel_spmd

F32 = mybir.dt.float32
BF16 = mybir.dt.bfloat16
AF = mybir.ActivationFunctionType
ALU = mybir.AluOpType
AX = mybir.AxisListType

D = 2048
T = 4096
MT = 512
NTILE = T // MT
NJ = MT // 128
H = 16
NG = 4
I0C = 44
I1C = 64
NCORE = 8

MODE = "full"
DEBUG = False
KTILES = NTILE
KSTAGES = 10**9
KLNPARTS = 99
STRICT_TR = True
KT2PARTS = 99

ENGINES = ("tensor", "vector", "scalar", "gpsimd", "sync")
DMA_SEMS = {"gpsimd": "dg", "sync": "ds", "scalar": "da"}


class Prog:
    def __init__(self, nc):
        self.nc = nc
        self.q = {e: [] for e in ENGINES}
        self.cnt = {s: 0 for s in ("pe", "dve", "act", "pool", "dg", "ds", "da")}
        self.sem = {}
        self.pending = {e: [] for e in ENGINES}
        self.enabled = True

    def emit(self, eng, fn, waits=(), inc=None, inc_n=1, tag=""):
        if not self.enabled:
            return self.cnt[inc] if inc is not None else None
        w = {}
        for s, c in list(self.pending[eng]) + list(waits):
            if c > 0:
                w[s] = max(w.get(s, 0), c)
        self.pending[eng] = []
        w = tuple(w.items())
        sem = self.sem
        def op(e, _fn=fn, _w=w, _inc=inc, _n=inc_n):
            for s, c in _w:
                e.wait_ge(sem[s], c)
            r = _fn(e)
            if _inc is not None and r is not None:
                r.then_inc(sem[_inc], _n)
        self.q[eng].append((op, w, inc, inc_n, tag))
        if inc is not None:
            self.cnt[inc] += inc_n
            return self.cnt[inc]
        return None

    def dma(self, eng, out_ap, in_ap, waits=(), tag=""):
        s = DMA_SEMS[eng]
        return self.emit(eng, lambda e, o=out_ap, i=in_ap: e.dma_start(out=o, in_=i),
                         waits=list(waits) + [(s, self.cnt[s])], inc=s, inc_n=16, tag=tag)

    def snap(self):
        return dict(self.cnt)

    def barrier(self, snap=None):
        c = snap or self.snap()
        for e in ENGINES:
            self.pending[e] = [(s, v) for s, v in c.items() if v > 0]
        return c

    def check(self):
        idx = {e: 0 for e in ENGINES}
        cur = {s: 0 for s in self.cnt}
        progress = True
        while progress:
            progress = False
            for e in ENGINES:
                while idx[e] < len(self.q[e]):
                    op, w, inc, inc_n, tag = self.q[e][idx[e]]
                    if any(cur[s] < c for s, c in w):
                        break
                    if inc is not None:
                        cur[inc] += inc_n
                    idx[e] += 1
                    progress = True
        stuck = {e: (idx[e], len(self.q[e])) for e in ENGINES if idx[e] < len(self.q[e])}
        if stuck:
            det = []
            for e, (i, n) in stuck.items():
                op, w, inc, inc_n, tag = self.q[e][i]
                miss = [(s, c, cur[s]) for s, c in w if cur[s] < c]
                det.append(f"{e}[{i}/{n}] tag={tag} missing={miss}")
            raise RuntimeError("build-time deadlock: " + "; ".join(det))
        for e in ENGINES:
            for op, w, inc, inc_n, tag in self.q[e]:
                for s, c in w:
                    if c > self.cnt[s]:
                        raise RuntimeError(f"wait beyond final: {e} tag={tag} {s}>={c} final={self.cnt[s]}")


def _tile_w(w, fo_slab):
    FO, FI = w.shape
    ns, c = FO // fo_slab, FI // 128
    a = np.asarray(w, dtype=np.float32).reshape(ns, fo_slab, c, 128)
    return np.ascontiguousarray(a.transpose(0, 3, 2, 1))


def _host_prepare(inp):
    W = {}
    W["wqkvu"] = _tile_w(np.asarray(inp["L0_attn_W_qkvu"]), 512)
    W["wo0"] = _tile_w(np.asarray(inp["L0_attn_W_out"]), 512)
    g0 = _tile_w(np.asarray(inp["L0_ffn_gate_w"]), 256)
    u0 = _tile_w(np.asarray(inp["L0_ffn_up_w"]), 256)
    W["wgu0"] = np.ascontiguousarray(np.concatenate([g0, u0], axis=3))
    W["wd0"] = _tile_w(np.asarray(inp["L0_ffn_down_w"]), 128)
    W["wq1"] = _tile_w(np.asarray(inp["L1_attn_Wq"]), 512)
    W["wk1"] = _tile_w(np.asarray(inp["L1_attn_Wk"]), 512)
    W["wv1"] = _tile_w(np.asarray(inp["L1_attn_Wv"]), 512)
    W["wgt1"] = _tile_w(np.asarray(inp["L1_attn_gate_w"]), 512)
    W["wo1"] = _tile_w(np.asarray(inp["L1_attn_Wo"]), 512)
    g1 = _tile_w(np.asarray(inp["L1_ffn_w_gate"]), 256)
    u1 = _tile_w(np.asarray(inp["L1_ffn_w_up"]), 256)
    W["wgu1"] = np.ascontiguousarray(np.concatenate([g1, u1], axis=3))
    W["wd1"] = _tile_w(np.asarray(inp["L1_ffn_w_down"]), 128)

    C = {}
    i = np.arange(128, dtype=np.float32)
    dd = i[None, :] - i[:, None]
    C["c_ddiag"] = np.where(dd >= 0, dd, 3.0e4).astype(np.float32)
    C["c_dqfull"] = np.broadcast_to(i[None, :] + 1.0, (128, 128)).astype(np.float32).copy()
    C["c_drev"] = (127.0 - i)[:, None].astype(np.float32).copy()
    C["c_eye"] = np.eye(128, dtype=np.float32)
    C["c_slope"] = np.asarray(inp["L0_attn_slope_rate"], np.float32).reshape(1, H).copy()
    C["c_gamma"] = np.asarray(inp["L1_attn_gamma"], np.float32).reshape(1, H).copy()
    C["c_gbrow"] = np.asarray(inp["L1_attn_gate_b"], np.float32).reshape(1, D).copy()
    C["c_lnwrow"] = np.asarray(inp["L1_attn_ln_w"], np.float32).reshape(1, D).copy()
    C["c_lnbrow"] = np.asarray(inp["L1_attn_ln_b"], np.float32).reshape(1, D).copy()
    cw = np.asarray(inp["L1_attn_conv_w"], np.float32).reshape(H, 128, 4)
    C["c_cw"] = np.ascontiguousarray(cw.transpose(1, 0, 2)).reshape(128, H * 4)
    cb = np.asarray(inp["L1_attn_conv_b"], np.float32).reshape(H, 128)
    C["c_cb"] = np.ascontiguousarray(cb.T)
    C["c_gs"] = np.ascontiguousarray(np.asarray(inp["L0_ffn_gate_s"], np.float32).reshape(I0C, 128).T)
    C["c_us"] = np.ascontiguousarray(np.asarray(inp["L0_ffn_up_s"], np.float32).reshape(I0C, 128).T)
    C["c_ds"] = np.ascontiguousarray(np.asarray(inp["L0_ffn_down_s"], np.float32).reshape(16, 128).T)
    return W, C


SLAB_SHAPES = {
    "wqkvu": (16, 16, 512), "wo0": (4, 16, 512), "wgu0": (22, 16, 512),
    "wd0": (16, 44, 128), "wq1": (4, 16, 512),
    "wk1": (4, 16, 512), "wv1": (4, 16, 512), "wgt1": (4, 16, 512),
    "wo1": (4, 16, 512), "wgu1": (32, 16, 512),
    "wd1": (16, 64, 128),
}
L0_W = ("wqkvu", "wo0", "wgu0", "wd0")


class WeightMgr:
    def __init__(self, p, wf32, wbf16, wsl):
        self.p = p
        self.wf32 = wf32
        self.wbf16 = wbf16
        self.wsl = wsl
        self.slot_free = [dict() for _ in wsl]
        self.turn = 0
        self.written_back = set()

    def load(self, name, s, pass0):
        p = self.p
        slot = self.turn
        self.turn = (self.turn + 1) % len(self.wsl)
        nsl, c, cols = SLAB_SHAPES[name]
        view = self.wsl[slot][:, 0:c * cols].rearrange("p (c f) -> p c f", c=c, f=cols)
        waits = list(self.slot_free[slot].items())
        key = (name, s)
        if pass0 and key not in self.written_back:
            n = p.dma("gpsimd", view, self.wf32[name][s], waits=waits, tag=f"ld0 {name}{s}")
            wb = p.dma("sync", self.wbf16[name][s], view, waits=[("dg", n)], tag=f"wb {name}{s}")
            self.written_back.add(key)
            return view, [("dg", n)], slot, [("ds", wb)]
        n = p.dma("sync", view, self.wbf16[name][s], waits=waits, tag=f"ld {name}{s}")
        return view, [("ds", n)], slot, []

    def release(self, slot, waits, extra=()):
        d = {}
        for s, c in list(waits) + list(extra):
            d[s] = max(d.get(s, 0), c)
        self.slot_free[slot] = d


def build_program(mode, debug=False):
    nc = bass.Bass("TRN2", target_bir_lowering=False)
    layers = {"l0": (0,), "l1": (1,), "full": (0, 1)}[mode]

    x_in = nc.dram_tensor("x", [T, D], F32, kind="ExternalInput").ap() if 0 in layers else None
    if mode == "full":
        x1 = nc.dram_tensor("x1buf", [T, D], F32, kind="Internal").ap()
        out = nc.dram_tensor("out", [T, D], F32, kind="ExternalOutput").ap()
    elif mode == "l0":
        x1 = nc.dram_tensor("out", [T, D], F32, kind="ExternalOutput").ap()
        out = None
    else:
        x1 = nc.dram_tensor("x1", [T, D], F32, kind="ExternalInput").ap()
        out = nc.dram_tensor("out", [T, D], F32, kind="ExternalOutput").ap()

    wf32, wbf16 = {}, {}
    for name, (nsl, c, cols) in SLAB_SHAPES.items():
        if ((0 in layers) if name in L0_W else (1 in layers)):
            wf32[name] = nc.dram_tensor(name, [nsl, 128, c, cols], F32, kind="ExternalInput").ap()
            wbf16[name] = nc.dram_tensor(name + "_bf", [nsl, 128, c, cols], BF16, kind="Internal").ap()

    cN = {"c_ddiag": [128, 128], "c_dqfull": [128, 128], "c_drev": [128, 1],
          "c_eye": [128, 128], "c_slope": [1, H], "c_gamma": [1, H],
          "c_gbrow": [1, D], "c_lnwrow": [1, D], "c_lnbrow": [1, D],
          "c_cw": [128, H * 4], "c_cb": [128, H], "c_gs": [128, I0C],
          "c_us": [128, I0C], "c_ds": [128, 16]}
    cin = {k: nc.dram_tensor(k, v, F32, kind="ExternalInput").ap() for k, v in cN.items()}

    dbg = {}
    def dbg_def(name, shape, dt=BF16):
        if debug:
            dbg[name] = nc.dram_tensor("dbg_" + name, [NTILE] + shape, dt, kind="ExternalOutput").ap()
    dbg_def("xn0", [128, H * 515]); dbg_def("s0", [128, NJ * D]); dbg_def("g0", [128, 16 * 512])
    dbg_def("o2", [128, NJ * D]); dbg_def("xn1", [128, H * 515]); dbg_def("xconv", [128, H * 512])
    dbg_def("kvtm", [128, NJ * D]); dbg_def("sfm", [128, H * 512]); dbg_def("omix", [128, NJ * D])
    dbg_def("s1", [128, NJ * D]); dbg_def("g1", [128, 16 * 512])

    stack = contextlib.ExitStack()
    sbuf = lambda name, shape, dt: stack.enter_context(nc.sbuf_tensor(name, shape, dt))
    psum = lambda name, shape, dt: stack.enter_context(nc.psum_tensor(name, shape, dt))

    with stack:
        xst0 = sbuf("xst0", [128, D], F32)
        xn = sbuf("xn", [128, H, 515], BF16)
        halo = sbuf("halo", [128, H, 3], BF16)
        poolA = sbuf("poolA", [128, 32768], BF16)
        poolB = sbuf("poolB", [128, 10240], BF16)
        poolC = sbuf("poolC", [128, 8192], BF16)
        poolD = sbuf("poolD", [128, 8192], BF16)
        poolE = sbuf("poolE", [128, 8192], BF16)
        kv_bf = sbuf("kv_bf", [128, D], BF16)
        s_state = sbuf("s_state", [128, H], F32)
        wsl = [sbuf("wsl0", [128, 8192], BF16)]
        id_bf = sbuf("id_bf", [128, 128], BF16)
        masks = sbuf("masks", [128, H, 128], BF16)
        qdt = sbuf("qdt", [128, H * 128], BF16)
        rbc = sbuf("rbc", [128, H], F32)
        gbc = sbuf("gbc", [128, H], F32)
        bdc = sbuf("bdc", [128, H], F32)
        kdc = sbuf("kdc", [128, H], F32)
        cw = sbuf("cw", [128, H, 4], F32)
        cb = sbuf("cb", [128, H], F32)
        gsc = sbuf("gsc", [128, I0C], F32)
        usc = sbuf("usc", [128, I0C], F32)
        dsc = sbuf("dsc", [128, 16], F32)
        gb_tbl = sbuf("gb_tbl", [128, D], BF16)
        lnw_tbl = sbuf("lnw_tbl", [128, D], BF16)
        lnb_tbl = sbuf("lnb_tbl", [128, D], BF16)
        scrA = sbuf("scrA", [128, 512], F32)
        scrB = sbuf("scrB", [128, 512], F32)
        scrC = sbuf("scrC", [128, 512], F32)
        sil_scr = sbuf("sil_scr", [128, 512], BF16)
        hstage = sbuf("hstage", [128, 512], BF16)
        stat = sbuf("stat", [128, 64], F32)
        nrm = sbuf("nrm", [128, NJ * H], F32)
        ssc = sbuf("ssc", [128, NJ, H], F32)
        onesf = scrB[0:1, 128:256]
        rowbuf = scrC[0:1, 0:512]
        dd_sb = scrA[:, 0:128]
        dq_sb = scrA[:, 128:256]
        drev_sb = scrA[:, 256:257]
        tmp128 = scrB[:, 0:128]
        ktk = [sil_scr[:, 256:384], sil_scr[:, 384:512]]
        qkm = [sil_scr[:, 0:128], sil_scr[:, 128:256]]
        sgt = scrC[:, 0:128]

        pb = [psum(f"pb{i}", [128, 512], F32) for i in range(4)]
        pqk_b = psum("pqk", [128, 512], F32)
        po_b = psum("po", [128, 512], F32)
        pkv_b = psum("pkv", [128, 512], F32)
        ptr = psum("ptr", [128, 1024], BF16)
        pqk = pqk_b[:, 0:128]
        po = po_b[:, 0:128]
        pkv = pkv_b[:, 0:128]
        pt_slot = [ptr[:, 0:128], ptr[:, 512:640]]

        sems = {}
        for s in ("pe", "dve", "act", "pool", "dg", "ds", "da"):
            sems[s] = stack.enter_context(nc.semaphore("sem_" + s))

        p = Prog(nc)
        p.sem = sems
        wm = WeightMgr(p, wf32, wbf16, wsl)
        tr_ev = [0, 0]
        tr_last = [0]

        vA = poolA[:]
        hdn = vA.rearrange("p (c f) -> p c f", c=64, f=512)
        slot = [vA[:, i * 8192:(i + 1) * 8192].rearrange("p (j f) -> p j f", j=NJ, f=D)
                for i in range(4)]
        grp = [poolB[:, i * 2048:(i + 1) * 2048].rearrange("p (j f) -> p j f", j=NJ, f=512)
               for i in range(5)]
        gate_tm = poolB[:, 0:8192].rearrange("p (j f) -> p j f", j=NJ, f=D)
        o2_tm = poolC[:].rearrange("p (j f) -> p j f", j=NJ, f=D)
        xconv = poolC[:].rearrange("p (c f) -> p c f", c=H, f=512)
        s_fm = poolC[:].rearrange("p (c f) -> p c f", c=H, f=512)
        y_bf = poolC[:].rearrange("p (j f) -> p j f", j=NJ, f=D)
        s_tm = poolD[:].rearrange("p (j f) -> p j f", j=NJ, f=D)
        g_fm = poolE[:].rearrange("p (c f) -> p c f", c=H, f=512)
        o2fm = g_fm
        omix_fm = g_fm
        xv = {}
        if x_in is not None:
            xv["x"] = x_in.rearrange("(n p) f -> n p f", p=128)
        xv["x1"] = x1.rearrange("(n p) f -> n p f", p=128)
        if out is not None:
            xv["out"] = out.rearrange("(n p) f -> n p f", p=128)

        def qd_rep(h):
            return bass.AP(qdt, h * 128, [[H * 128, 128], [0, NJ], [1, 128]])

        def junkD(off, width):
            return bass.AP(poolD, off, [[8192, 128], [1, width]])

        def dbg_dump(name, src_ap, wait_sem, it):
            if debug and name in dbg:
                p.emit("scalar", lambda e, it=it, s=src_ap: e.dma_start(out=dbg[name][it], in_=s),
                       waits=[(wait_sem, p.cnt[wait_sem]), ("da", p.cnt["da"])],
                       inc="da", inc_n=16, tag="dbg " + name)
                p.barrier()

        class _StopEmit(Exception):
            pass
        stage_i = [0]

        def gate():
            stage_i[0] += 1
            if stage_i[0] > KSTAGES:
                p.enabled = False

        for k, ap in cin.items():
            tgt = {"c_ddiag": dd_sb, "c_dqfull": dq_sb, "c_drev": drev_sb,
                   "c_cw": cw[:], "c_cb": cb[:], "c_gs": gsc[:], "c_us": usc[:], "c_ds": dsc[:]}.get(k)
            if k == "c_eye":
                p.dma("gpsimd", id_bf[:], ap, tag="eye")
            elif tgt is not None:
                p.dma("gpsimd", tgt, ap, tag=k)
        dg_consts = p.cnt["dg"]

        for buf in (kv_bf, s_state, halo, xn):
            p.emit("gpsimd", lambda e, b=buf: e.memset(b[:], 0.0), inc="pool", tag="memset")
        p.emit("gpsimd", lambda e: e.memset(onesf, 1.0), inc="pool", tag="ones")
        pool_init = p.cnt["pool"]

        gate()

        def bcast_row(src_ap, width, dst, tag):
            for b0 in range(0, width, 512):
                w = min(512, width - b0)
                nd = p.dma("gpsimd", rowbuf[0:1, 0:w], src_ap[0:1, b0:b0 + w],
                           waits=[("pe", p.cnt["pe"])], tag="row " + tag)
                nmm = p.emit("tensor",
                             lambda e, w=w: e.matmul(pb[0][:, 0:w], onesf[:],
                                                     rowbuf[:, 0:w], start=True, stop=True),
                             waits=[("dg", nd), ("pool", pool_init), ("dve", p.cnt["dve"])],
                             inc="pe", tag="bc mm " + tag)
                p.emit("vector",
                       lambda e, b0=b0, w=w, d=dst: e.tensor_copy(d[:, b0:b0 + w], pb[0][:, 0:w]),
                       waits=[("pe", nmm)], inc="dve", tag="bc ev " + tag)

        bcast_row(cin["c_slope"], H, rbc, "slope")
        gate()
        if 1 in layers:
            bcast_row(cin["c_gamma"], H, gbc, "gamma")
            bcast_row(cin["c_gbrow"], D, gb_tbl, "gb")
            bcast_row(cin["c_lnwrow"], D, lnw_tbl, "lnw")
            bcast_row(cin["c_lnbrow"], D, lnb_tbl, "lnb")

        gate()
        if 0 in layers:
            c0 = p.snap()
            n1 = p.emit("vector", lambda e: e.tensor_scalar(stat[:, 0:H], rbc[:], -128.0, None, ALU.mult),
                        waits=[("dve", c0["dve"])], inc="dve", tag="bd pre")
            p.emit("scalar", lambda e: e.activation(bdc[:], stat[:, 0:H], AF.Exp),
                   waits=[("dve", n1)], inc="act", tag="bd exp")
            drev_b = bass.AP(scrA, 256, [[512, 128], [0, H]])
            n2 = p.emit("vector", lambda e: e.tensor_tensor(stat[:, H:2 * H], rbc[:], drev_b, ALU.mult),
                        waits=[("dve", c0["dve"]), ("dg", dg_consts)], inc="dve", tag="kd pre")
            p.emit("scalar", lambda e: e.activation(kdc[:], stat[:, H:2 * H], AF.Exp, scale=-1.0),
                   waits=[("dve", n2)], inc="act", tag="kd exp")
            for h in range(H):
                nm = p.emit("vector",
                            lambda e, h=h: e.tensor_scalar(tmp128, dd_sb, rbc[:, h:h + 1], None, ALU.mult),
                            waits=[("dg", dg_consts), ("act", p.cnt["act"])], inc="dve", tag=f"mpre {h}")
                p.emit("scalar", lambda e, h=h: e.activation(masks[:, h, :], tmp128, AF.Exp, scale=-1.0),
                       waits=[("dve", nm)], inc="act", tag=f"mexp {h}")
                nq = p.emit("vector",
                            lambda e, h=h: e.tensor_scalar(tmp128, dq_sb, rbc[:, h:h + 1], None, ALU.mult),
                            waits=[("act", p.cnt["act"])], inc="dve", tag=f"qpre {h}")
                p.emit("scalar", lambda e, h=h: e.activation(
                    bass.AP(qdt, h * 128, [[H * 128, 128], [1, 128]]), tmp128, AF.Exp, scale=-1.0),
                       waits=[("dve", nq)], inc="act", tag=f"qexp {h}")

        gate()
        p.barrier()

        bank_free = {id(b): {} for b in pb}

        def bfree(bank, **kw):
            d = bank_free[id(bank)]
            for k, v in kw.items():
                d[k] = max(d.get(k, 0), v)

        def ln_stage(it, src):
            en = p.enabled
            for jj in range(NJ):
                c0 = p.snap()
                nl = p.dma("gpsimd", xst0[:], xv[src][it * NJ + jj],
                           waits=[("dve", c0["dve"]), ("act", c0["act"]),
                                  ("pe", c0["pe"]), ("da", c0["da"])],
                           tag=f"ln ld {jj}")
                if KLNPARTS < 2:
                    p.enabled = False
                na = p.emit("scalar",
                            lambda e, jj=jj: e.activation(junkD(jj * 2048, 2048), xst0[:], AF.Square,
                                                          accum_out=stat[:, jj:jj + 1]),
                            waits=[("dg", nl), ("dve", c0["dve"])], inc="act", tag=f"ln sq {jj}")
                ns = p.emit("vector", lambda e, jj=jj: e.reduce_sum(stat[:, 8 + jj:9 + jj], xst0[:], axis=AX.X),
                            waits=[("dg", nl)], inc="dve", tag=f"ln sum {jj}")
                if KLNPARTS < 3:
                    p.enabled = False
                nv1 = p.emit("vector", lambda e, jj=jj: e.tensor_scalar(
                    stat[:, 16 + jj:17 + jj], stat[:, 8 + jj:9 + jj], 1.0 / D, None, ALU.mult),
                             waits=[("dve", ns)], inc="dve", tag="ln mean")
                nv2 = p.emit("vector", lambda e, jj=jj: e.tensor_tensor(
                    stat[:, 24 + jj:25 + jj], stat[:, 16 + jj:17 + jj], stat[:, 16 + jj:17 + jj], ALU.mult),
                             waits=[("dve", nv1)], inc="dve", tag="ln m2")
                nv3 = p.emit("vector", lambda e, jj=jj: e.scalar_tensor_tensor(
                    stat[:, 32 + jj:33 + jj], stat[:, jj:jj + 1], 1.0 / D, stat[:, 24 + jj:25 + jj],
                    ALU.mult, ALU.subtract),
                             waits=[("act", na), ("dve", nv2)], inc="dve", tag="ln var")
                nv4 = p.emit("vector", lambda e, jj=jj: e.tensor_scalar(
                    stat[:, 40 + jj:41 + jj], stat[:, 32 + jj:33 + jj], 1e-5, None, ALU.add),
                             waits=[("dve", nv3)], inc="dve", tag="ln eps")
                nv5 = p.emit("vector", lambda e, jj=jj: e.reciprocal(stat[:, 48 + jj:49 + jj], stat[:, 40 + jj:41 + jj]),
                             waits=[("dve", nv4)], inc="dve", tag="ln recip")
                na2 = p.emit("scalar", lambda e, jj=jj: e.activation(
                    stat[:, 56 + jj:57 + jj], stat[:, 48 + jj:49 + jj], AF.Sqrt),
                             waits=[("dve", nv5)], inc="act", tag="ln sqrt")
                if KLNPARTS < 4:
                    p.enabled = False
                nap = p.emit("vector",
                             lambda e, jj=jj: e.tensor_scalar(s_tm[:, jj, :], xst0[:],
                                                              stat[:, 16 + jj:17 + jj], stat[:, 56 + jj:57 + jj],
                                                              ALU.subtract, ALU.mult),
                             waits=[("act", na2)], inc="dve", tag=f"ln apply {jj}")
                if KLNPARTS < 5:
                    p.enabled = False
                for c in range(H):
                    sl = (jj * H + c) % 2
                    ntr = p.emit("tensor",
                                 lambda e, jj=jj, c=c, sl=sl: e.transpose(
                                     pt_slot[sl], s_tm[:, jj, c * 128:(c + 1) * 128], id_bf[:]),
                                 waits=[("dve", nap), ("dve", tr_ev[1] if STRICT_TR else tr_ev[sl]),
                                        ("dve", tr_ev[0] if STRICT_TR else 0)],
                                 inc="pe", tag=f"lntr {jj}.{c}")
                    tr_ev[sl] = p.emit("vector",
                                       lambda e, jj=jj, c=c, sl=sl: e.tensor_copy(
                                           xn[:, c, 3 + jj * 128:3 + (jj + 1) * 128], pt_slot[sl]),
                                       waits=[("pe", ntr)], inc="dve", tag=f"lnev {jj}.{c}")
                p.enabled = en

        def transpose_set(get_src, get_dst, n_c, n_j, tag, waits_extra=()):
            for c in range(n_c):
                for jj in range(n_j):
                    sl = (c * n_j + jj) % 2
                    s_ap = get_src(c, jj)
                    d_ap = get_dst(c, jj)
                    ntr = p.emit("tensor",
                                 lambda e, s_ap=s_ap, sl=sl: e.transpose(pt_slot[sl], s_ap, id_bf[:]),
                                 waits=list(waits_extra) + [("dve", tr_ev[0]), ("dve", tr_ev[1])],
                                 inc="pe", tag=f"tr {tag} {c}.{jj}")
                    tr_ev[sl] = p.emit("vector",
                                       lambda e, d_ap=d_ap, sl=sl: e.tensor_copy(d_ap, pt_slot[sl]),
                                       waits=[("pe", ntr)], inc="dve", tag=f"trev {tag}")

        def proj_tm(w_name, s_range, lhs3, lhs_off, out_cb, pass0, tag, lhs_waits):
            for s in s_range:
                view, wws, slotid, extra = wm.load(w_name, s, pass0)
                last_pe = 0
                for jj in range(NJ):
                    bank = pb[jj % 2 + (2 if (s % 2) else 0)]
                    def mmf(e, view=view, jj=jj, bank=bank):
                        r = None
                        for fi in range(H):
                            r = e.matmul(bank[:], lhs3[:, fi, lhs_off + jj * 128:lhs_off + (jj + 1) * 128],
                                         view[:, fi, :], start=(fi == 0), stop=(fi == H - 1))
                        return r
                    nmm = p.emit("tensor", mmf,
                                 waits=wws + list(lhs_waits) + list(bank_free[id(bank)].items()),
                                 inc="pe", tag=f"{tag} mm s{s} j{jj}")
                    last_pe = nmm
                    out_cb(jj, s, bank, nmm)
                wm.release(slotid, [("pe", last_pe)], extra)

        def resid_stage(it, src, dst, g_ready_act, s_ready_dve):
            nonlocal xst1_store
            prev_last = 0
            for jj in range(NJ):
                nl = p.dma("gpsimd", xst0[:], xv[src][it * NJ + jj],
                           waits=[("dve", p.cnt["dve"]), ("da", p.cnt["da"])], tag=f"res ld {jj}")
                chunk = [prev_last]
                for c in range(H):
                    sl = c % 2
                    ntr = p.emit("tensor",
                                 lambda e, c=c, jj=jj, sl=sl: e.transpose(
                                     pt_slot[sl], g_fm[:, c, jj * 128:(jj + 1) * 128], id_bf[:]),
                                 waits=[("act", g_ready_act), ("dve", tr_ev[0]), ("dve", tr_ev[1])],
                                 inc="pe", tag=f"gtr {jj}.{c}")
                    nsg = p.emit("vector",
                                 lambda e, c=c, jj=jj, sl=sl: e.tensor_tensor(
                                     sgt, pt_slot[sl], s_tm[:, jj, c * 128:(c + 1) * 128], ALU.mult),
                                 waits=[("pe", ntr), ("dve", s_ready_dve), ("dve", chunk[-1])],
                                 inc="dve", tag=f"sg {jj}.{c}")
                    tr_ev[sl] = nsg
                    nxo = p.emit("vector",
                                 lambda e, c=c: e.tensor_tensor(
                                     xst0[:, c * 128:(c + 1) * 128], xst0[:, c * 128:(c + 1) * 128],
                                     sgt, ALU.add),
                                 waits=[("dve", nsg), ("dg", nl)],
                                 inc="dve", tag=f"xo {jj}.{c}")
                    chunk.append(nxo)
                prev_last = chunk[-1]
                xst1_store = p.emit("scalar",
                                    lambda e, jj=jj, dst=dst, it=it: e.dma_start(out=xv[dst][it * NJ + jj], in_=xst0[:]),
                                    waits=[("dve", max(chunk)), ("da", p.cnt["da"])],
                                    inc="da", inc_n=16, tag=f"res st {jj}")

        xst1_store = 0

        kvbf_cnt = {h: 0 for h in range(H)}

        def _tile_loop_body():
          nonlocal xst1_store
          for it in range(KTILES):
            pass0 = (it == 0)
            if 0 in layers:
                gate()
                ln_stage(it, "x")
                xn_done = p.cnt["dve"]
                dbg_dump("xn0", xn[:].rearrange("p a b -> p (a b)"), "dve", it)

                gate()
                q_sil, qqd, k_sil, v_tm, u_tm = grp
                for g in range(NG):
                    def proj_fm(w_name, s, out_cb, tag):
                        view, wws, slotid, extra = wm.load(w_name, s, pass0)
                        last = 0
                        for j in range(4):
                            bank = pb[j % 2]
                            def mmf(e, view=view, j=j, bank=bank):
                                r = None
                                for fi in range(H):
                                    r = e.matmul(bank[:], view[:, fi, j * 128:(j + 1) * 128],
                                                 xn[:, fi, 3:515], start=(fi == 0), stop=(fi == H - 1))
                                return r
                            nmm = p.emit("tensor", mmf,
                                         waits=wws + [("dve", xn_done)] + list(bank_free[id(bank)].items()),
                                         inc="pe", tag=f"{tag} mm {j}")
                            last = nmm
                            out_cb(j, s, bank, nmm)
                        wm.release(slotid, [("pe", last)], extra)

                    def q_out(j, s, bank, nmm, g=g):
                        h = 4 * g + j
                        na = p.emit("scalar",
                                    lambda e, j=j, bank=bank: e.activation(q_sil[:, j, :], bank[:], AF.Silu),
                                    waits=[("pe", nmm)], inc="act", tag=f"q sil {h}")
                        p.emit("vector",
                               lambda e, j=j, h=h: e.tensor_tensor(
                                   qqd[:, j, :].rearrange("p (r f) -> p r f", r=NJ),
                                   q_sil[:, j, :].rearrange("p (r f) -> p r f", r=NJ),
                                   qd_rep(h), ALU.mult),
                               waits=[("act", na)], inc="dve", tag=f"qqd {h}")
                        bfree(bank, act=na)
                    proj_fm("wqkvu", g, q_out, f"q g{g}")

                    def k_out(j, s, bank, nmm, g=g):
                        na = p.emit("scalar",
                                    lambda e, j=j, bank=bank: e.activation(k_sil[:, j, :], bank[:], AF.Silu),
                                    waits=[("pe", nmm)], inc="act", tag=f"k sil g{g}.{j}")
                        bfree(bank, act=na)
                    proj_fm("wqkvu", 4 + g, k_out, f"k g{g}")

                    def v_out(jj, s, bank, nmm):
                        nv = p.emit("vector", lambda e, jj=jj, bank=bank: e.tensor_copy(v_tm[:, jj, :], bank[:]),
                                    waits=[("pe", nmm)], inc="dve", tag=f"v ev {jj}")
                        bfree(bank, dve=nv)
                    proj_tm("wqkvu", [8 + g], xn, 3, v_out, pass0, f"v g{g}", [("dve", xn_done)])

                    def u_out(jj, s, bank, nmm):
                        nv = p.emit("vector", lambda e, jj=jj, bank=bank: e.tensor_copy(u_tm[:, jj, :], bank[:]),
                                    waits=[("pe", nmm)], inc="dve", tag=f"u ev {jj}")
                        bfree(bank, dve=nv)
                    proj_tm("wqkvu", [12 + g], xn, 3, u_out, pass0, f"u g{g}", [("dve", xn_done)])

                    grp_ready = p.snap()
                    po_free = {"act": 0, "dve": 0}
                    pqk_free = 0
                    pkv_free = 0
                    en2 = p.enabled
                    for blk in range(NJ):
                        for hl in range(4):
                            h = 4 * g + hl
                            sl = (blk * 4 + hl) % 2
                            if KT2PARTS < 2:
                                p.enabled = False
                            ntr = p.emit("tensor",
                                         lambda e, hl=hl, blk=blk, sl=sl: e.transpose(
                                             pt_slot[sl], k_sil[:, hl, blk * 128:(blk + 1) * 128], id_bf[:]),
                                         waits=[("act", grp_ready["act"]), ("dve", tr_ev[0]), ("dve", tr_ev[1])],
                                         inc="pe", tag=f"ktr {h}.{blk}")
                            nkd = p.emit("vector",
                                         lambda e, h=h, sl=sl: e.tensor_scalar(
                                             ktk[sl], pt_slot[sl], kdc[:, h:h + 1], None, ALU.mult),
                                         waits=[("pe", ntr)], inc="dve", tag=f"ktk {h}.{blk}")
                            tr_ev[sl] = nkd
                            if KT2PARTS < 3:
                                p.enabled = False
                            nqk = p.emit("tensor",
                                         lambda e, hl=hl, blk=blk: e.matmul(
                                             pqk, k_sil[:, hl, blk * 128:(blk + 1) * 128],
                                             q_sil[:, hl, blk * 128:(blk + 1) * 128], start=True, stop=True),
                                         waits=[("dve", pqk_free)], inc="pe", tag=f"qk {h}.{blk}")
                            nmsk = p.emit("vector",
                                          lambda e, h=h, sl=sl: e.scalar_tensor_tensor(
                                              qkm[sl], masks[:, h, :], 1.0, pqk, ALU.bypass, ALU.mult),
                                          waits=[("pe", nqk)], inc="dve", tag=f"mask {h}.{blk}")
                            pqk_free = nmsk
                            if KT2PARTS < 4:
                                p.enabled = False
                            def o_mm(e, hl=hl, blk=blk, h=h, sl=sl):
                                e.matmul(po, qqd[:, hl, blk * 128:(blk + 1) * 128],
                                         kv_bf[:, h * 128:(h + 1) * 128], start=True, stop=False)
                                return e.matmul(po, qkm[sl], v_tm[:, blk, hl * 128:(hl + 1) * 128],
                                                start=False, stop=True)
                            nomm = p.emit("tensor", o_mm,
                                          waits=[("dve", nmsk), ("dve", kvbf_cnt[h]),
                                                 ("act", po_free["act"]), ("dve", po_free["dve"])],
                                          inc="pe", tag=f"o {h}.{blk}")
                            if KT2PARTS < 5:
                                p.enabled = False
                            nkvm = p.emit("tensor",
                                          lambda e, blk=blk, hl=hl, sl=sl: e.matmul(
                                              pkv, ktk[sl], v_tm[:, blk, hl * 128:(hl + 1) * 128],
                                              start=True, stop=True),
                                          waits=[("dve", nkd), ("dve", pkv_free)], inc="pe", tag=f"kvmm {h}.{blk}")
                            if KT2PARTS < 6:
                                p.enabled = False
                            no2 = p.emit("vector",
                                         lambda e, blk=blk, hl=hl, h=h: e.scalar_tensor_tensor(
                                             o2_tm[:, blk, h * 128:(h + 1) * 128],
                                             u_tm[:, blk, hl * 128:(hl + 1) * 128], 1.0,
                                             po, ALU.bypass, ALU.mult),
                                         waits=[("pe", nomm)], inc="dve", tag=f"o2 {h}.{blk}")
                            nss = p.emit("scalar",
                                         lambda e, blk=blk, h=h: e.activation(
                                             junkD(blk * 2048 + ((h * 128) % 2048), 128),
                                             po, AF.Square, accum_out=ssc[:, blk, h:h + 1]),
                                         waits=[("pe", nomm), ("dve", no2)], inc="act", tag=f"ss {h}.{blk}")
                            po_free = {"act": nss, "dve": no2}
                            if KT2PARTS < 7:
                                p.enabled = False
                            nkv1 = p.emit("vector",
                                          lambda e, h=h: e.scalar_tensor_tensor(
                                              kv_bf[:, h * 128:(h + 1) * 128], kv_bf[:, h * 128:(h + 1) * 128],
                                              bdc[:, h:h + 1], pkv, ALU.mult, ALU.add),
                                          waits=[("pe", nkvm)], inc="dve", tag=f"kvup {h}.{blk}")
                            pkv_free = nkv1
                            kvbf_cnt[h] = nkv1
                            p.enabled = en2

                gate()
                c1 = p.snap()
                nred = p.emit("vector", lambda e: e.reduce_sum(stat[:, 0:NJ], ssc[:], axis=AX.X),
                              waits=[("act", c1["act"])], inc="dve", tag="rms red")
                nms = p.emit("vector", lambda e: e.tensor_scalar(stat[:, 8:8 + NJ], stat[:, 0:NJ],
                                                                 1.0 / D, 1e-6, ALU.mult, ALU.add),
                             waits=[("dve", nred)], inc="dve", tag="rms ms")
                nrc = p.emit("vector", lambda e: e.reciprocal(stat[:, 16:16 + NJ], stat[:, 8:8 + NJ]),
                             waits=[("dve", nms)], inc="dve", tag="rms recip")
                nsq = p.emit("scalar", lambda e: e.activation(stat[:, 24:24 + NJ], stat[:, 16:16 + NJ], AF.Sqrt),
                             waits=[("dve", nrc)], inc="act", tag="rms sqrt")
                o2s = []
                for blk in range(NJ):
                    o2s.append(p.emit("vector",
                                      lambda e, blk=blk: e.tensor_scalar(
                                          o2_tm[:, blk, :], o2_tm[:, blk, :],
                                          stat[:, 24 + blk:25 + blk], None, ALU.mult),
                                      waits=[("act", nsq), ("dve", c1["dve"])], inc="dve", tag=f"o2s {blk}"))
                dbg_dump("o2", poolC[:], "dve", it)

                transpose_set(lambda c, jj: o2_tm[:, jj, c * 128:(c + 1) * 128],
                              lambda c, jj: o2fm[:, c, jj * 128:(jj + 1) * 128],
                              H, NJ, "o2", waits_extra=[("dve", max(o2s))])
                o2fm_done = p.cnt["dve"]

                def wout_out(jj, s, bank, nmm):
                    nv = p.emit("vector",
                                lambda e, jj=jj, s=s, bank=bank: e.tensor_copy(
                                    s_tm[:, jj, s * 512:(s + 1) * 512], bank[:]),
                                waits=[("pe", nmm)], inc="dve", tag=f"s ev {jj}.{s}")
                    bfree(bank, dve=nv)
                proj_tm("wo0", range(4), o2fm, 0, wout_out, pass0, "wo0", [("dve", o2fm_done)])
                s0_done = p.cnt["dve"]
                dbg_dump("s0", poolD[:], "dve", it)

                gate()
                prev_hdn = 0
                for cs in range(22):
                    view, wws, slotid, extra = wm.load("wgu0", cs, pass0)
                    last = 0
                    for k in range(2):
                        fo = 2 * cs + k
                        bg, bu = pb[2 * (k % 2)], pb[2 * (k % 2) + 1]
                        def gmm(e, view=view, k=k, bg=bg):
                            r = None
                            for fi in range(H):
                                r = e.matmul(bg[:], view[:, fi, k * 128:(k + 1) * 128], xn[:, fi, 3:515],
                                             start=(fi == 0), stop=(fi == H - 1))
                            return r
                        ng = p.emit("tensor", gmm,
                                    waits=wws + [("dve", xn_done)] + list(bank_free[id(bg)].items()),
                                    inc="pe", tag=f"f0g {fo}")
                        def umm(e, view=view, k=k, bu=bu):
                            r = None
                            for fi in range(H):
                                r = e.matmul(bu[:], view[:, fi, 256 + k * 128:256 + (k + 1) * 128],
                                             xn[:, fi, 3:515], start=(fi == 0), stop=(fi == H - 1))
                            return r
                        nu = p.emit("tensor", umm,
                                    waits=wws + list(bank_free[id(bu)].items()),
                                    inc="pe", tag=f"f0u {fo}")
                        last = nu
                        na = p.emit("scalar",
                                    lambda e, fo=fo, bg=bg: e.activation(sil_scr[:], bg[:], AF.Silu,
                                                                         scale=gsc[:, fo:fo + 1]),
                                    waits=[("pe", ng), ("dve", prev_hdn)], inc="act", tag=f"f0sil {fo}")
                        prev_hdn = p.emit("vector",
                                          lambda e, fo=fo, bu=bu: e.scalar_tensor_tensor(
                                              hdn[:, fo, :], bu[:], usc[:, fo:fo + 1], sil_scr[:],
                                              ALU.mult, ALU.mult),
                                          waits=[("act", na), ("pe", nu)], inc="dve", tag=f"f0hdn {fo}")
                        bfree(bg, act=na)
                        bfree(bu, dve=prev_hdn)
                    wm.release(slotid, [("pe", last)], extra)
                hdn_all = p.cnt["dve"]

                gate()
                g0_last = 0
                for s in range(16):
                    view, wws, slotid, extra = wm.load("wd0", s, pass0)
                    bank = pb[s % 2]
                    def dmm(e, view=view, bank=bank):
                        r = None
                        for c in range(I0C):
                            r = e.matmul(bank[:], view[:, c, :], hdn[:, c, :],
                                         start=(c == 0), stop=(c == I0C - 1))
                        return r
                    nmm = p.emit("tensor", dmm,
                                 waits=wws + [("dve", hdn_all)] + list(bank_free[id(bank)].items()),
                                 inc="pe", tag=f"d0 {s}")
                    g0_last = p.emit("scalar",
                                     lambda e, s=s, bank=bank: e.activation(g_fm[:, s, :], bank[:], AF.Silu,
                                                                            scale=dsc[:, s:s + 1]),
                                     waits=[("pe", nmm)], inc="act", tag=f"d0sil {s}")
                    bfree(bank, act=g0_last)
                    wm.release(slotid, [("pe", nmm)], extra)
                dbg_dump("g0", poolE[:], "act", it)

                gate()
                resid_stage(it, "x", "x1", g0_last, s0_done)

            if 1 in layers:
                gate()
                cpre = p.snap()
                p.emit("vector", lambda e: e.tensor_copy(xn[:, :, 0:3], halo[:]),
                       waits=[("pe", cpre["pe"]), ("pool", p.cnt["pool"])],
                       inc="dve", tag="halo in")
                ln_stage(it, "x1")
                p.emit("vector", lambda e: e.tensor_copy(halo[:], xn[:, :, 512:515]),
                       waits=[("dve", p.cnt["dve"])], inc="dve", tag="halo out")
                xn_done = p.cnt["dve"]
                dbg_dump("xn1", xn[:].rearrange("p a b -> p (a b)"), "dve", it)

                gate()
                prev_sil = 0
                for c in range(H):
                    n1 = p.emit("vector",
                                lambda e, c=c: e.tensor_scalar(scrA[:, 0:512], xn[:, c, 0:512],
                                                               cw[:, c, 0:1], None, ALU.mult),
                                waits=[("dve", xn_done), ("act", prev_sil)], inc="dve", tag=f"cv0 {c}")
                    for k in range(1, 4):
                        n1 = p.emit("vector",
                                    lambda e, c=c, k=k: e.scalar_tensor_tensor(
                                        scrA[:, 0:512], xn[:, c, k:k + 512], cw[:, c, k:k + 1],
                                        scrA[:, 0:512], ALU.mult, ALU.add),
                                    waits=[("dve", n1)], inc="dve", tag=f"cv{k} {c}")
                    prev_sil = p.emit("scalar",
                                      lambda e, c=c: e.activation(xconv[:, c, :], scrA[:, 0:512], AF.Silu,
                                                                  bias=cb[:, c:c + 1]),
                                      waits=[("dve", n1)], inc="act", tag=f"cvsil {c}")
                xconv_done = prev_sil
                dbg_dump("xconv", poolC[:], "act", it)

                gate()
                k_n, v_b, kv_tm, q_n = slot

                def norm_proj(w_name, dst, tag):
                    red_cnt = [0]
                    def cb_(jj, s, bank, nmm, dst=dst):
                        nv = p.emit("vector",
                                    lambda e, jj=jj, s=s, bank=bank: e.tensor_copy(
                                        dst[:, jj, s * 512:(s + 1) * 512], bank[:]),
                                    waits=[("pe", nmm)], inc="dve", tag=f"{tag} ev {jj}.{s}")
                        na = p.emit("scalar",
                                    lambda e, bank=bank: e.activation(scrB[:], bank[:], AF.Square),
                                    waits=[("pe", nmm), ("dve", red_cnt[-1]), ("dve", nv)],
                                    inc="act", tag=f"{tag} sq")
                        nr = p.emit("vector",
                                    lambda e, jj=jj, s=s: e.reduce_sum(
                                        nrm[:, jj * H + s * 4:jj * H + s * 4 + 4],
                                        scrB[:].rearrange("p (a b) -> p a b", a=4), axis=AX.X),
                                    waits=[("act", na)], inc="dve", tag=f"{tag} red")
                        red_cnt.append(nr)
                        bfree(bank, dve=nv, act=na)
                    proj_tm(w_name, range(4), xn, 3, cb_, pass0, tag, [("dve", xn_done)])
                    nq1 = p.emit("scalar", lambda e: e.activation(stat[:, 0:NJ * H], nrm[:], AF.Sqrt),
                                 waits=[("dve", red_cnt[-1])], inc="act", tag=f"{tag} nsq")
                    nq2 = p.emit("vector", lambda e: e.tensor_scalar(stat[:, 0:NJ * H], stat[:, 0:NJ * H],
                                                                     1e-12, None, ALU.max),
                                 waits=[("act", nq1)], inc="dve", tag=f"{tag} nmax")
                    nq3 = p.emit("vector", lambda e: e.reciprocal(nrm[:], stat[:, 0:NJ * H]),
                                 waits=[("dve", nq2)], inc="dve", tag=f"{tag} nrec")
                    last = 0
                    for jj in range(NJ):
                        for h in range(H):
                            last = p.emit("vector",
                                          lambda e, jj=jj, h=h, dst=dst: e.tensor_scalar(
                                              dst[:, jj, h * 128:(h + 1) * 128],
                                              dst[:, jj, h * 128:(h + 1) * 128],
                                              nrm[:, jj * H + h:jj * H + h + 1], None, ALU.mult),
                                          waits=[("dve", nq3)], inc="dve", tag=f"{tag} ap {jj}.{h}")
                    return last

                k_done = norm_proj("wk1", k_n, "k1")
                def v_out1(jj, s, bank, nmm):
                    nv = p.emit("vector",
                                lambda e, jj=jj, s=s, bank=bank: e.tensor_copy(
                                    v_b[:, jj, s * 512:(s + 1) * 512], bank[:]),
                                waits=[("pe", nmm)], inc="dve", tag=f"v1 ev {jj}.{s}")
                    bfree(bank, dve=nv)
                proj_tm("wv1", range(4), xn, 3, v_out1, pass0, "v1", [("dve", xn_done)])
                v_done = p.cnt["dve"]
                kv_done = []
                for jj in range(NJ):
                    kv_done.append(p.emit("vector",
                                          lambda e, jj=jj: e.tensor_tensor(
                                              kv_tm[:, jj, :], k_n[:, jj, :], v_b[:, jj, :], ALU.mult),
                                          waits=[("dve", max(k_done, v_done))], inc="dve", tag=f"kv1 {jj}"))
                dbg_dump("kvtm", poolA[:, 16384:24576], "dve", it)
                q_done = norm_proj("wq1", q_n, "q1")

                def gate_out(jj, s, bank, nmm):
                    nv = p.emit("vector",
                                lambda e, jj=jj, s=s, bank=bank: e.scalar_tensor_tensor(
                                    scrC[:], gb_tbl[:, s * 512:(s + 1) * 512], 1.0, bank[:],
                                    ALU.bypass, ALU.add),
                                waits=[("pe", nmm), ("act", p.cnt["act"])], inc="dve", tag=f"ga {jj}.{s}")
                    na = p.emit("scalar",
                                lambda e, jj=jj, s=s: e.activation(
                                    gate_tm[:, jj, s * 512:(s + 1) * 512], scrC[:], AF.Sigmoid),
                                waits=[("dve", nv)], inc="act", tag=f"gs {jj}.{s}")
                    bfree(bank, dve=nv)
                proj_tm("wgt1", range(4), xconv, 0, gate_out, pass0, "gt1", [("act", xconv_done)])
                gate_done = p.cnt["act"]

                gate()
                scol = []
                for c in range(H):
                    evs = []
                    for jj in range(NJ):
                        sl = (c * NJ + jj) % 2
                        ntr = p.emit("tensor",
                                     lambda e, c=c, jj=jj, sl=sl: e.transpose(
                                         pt_slot[sl], kv_tm[:, jj, c * 128:(c + 1) * 128], id_bf[:]),
                                     waits=[("dve", max(kv_done)), ("dve", tr_ev[0]), ("dve", tr_ev[1])],
                                     inc="pe", tag=f"kvtr {c}.{jj}")
                        ne = p.emit("vector",
                                    lambda e, jj=jj, sl=sl: e.tensor_copy(
                                        scrB[:, jj * 128:(jj + 1) * 128], pt_slot[sl]),
                                    waits=[("pe", ntr), ("act", gate_done)], inc="dve", tag=f"kvev {c}.{jj}")
                        tr_ev[sl] = ne
                        evs.append(ne)
                    g_ap = bass.AP(gbc, c, [[H, 128], [0, 512]])
                    nsc = p.emit("vector",
                                 lambda e, c=c, g_ap=g_ap: e.tensor_tensor_scan(
                                     scrC[:], g_ap, scrB[:], s_state[:, c:c + 1], ALU.mult, ALU.add),
                                 waits=[("dve", max(evs))], inc="dve", tag=f"scan {c}")
                    p.emit("vector", lambda e, c=c: e.tensor_copy(s_state[:, c:c + 1], scrC[:, 511:512]),
                           waits=[("dve", nsc)], inc="dve", tag=f"scan st {c}")
                    scol.append(p.emit("vector", lambda e, c=c: e.tensor_copy(s_fm[:, c, :], scrC[:]),
                                       waits=[("dve", nsc)], inc="dve", tag=f"scan cp {c}"))
                dbg_dump("sfm", poolC[:], "dve", it)

                transpose_set(lambda c, jj: s_fm[:, c, jj * 128:(jj + 1) * 128],
                              lambda c, jj: v_b[:, jj, c * 128:(c + 1) * 128],
                              H, NJ, "sT", waits_extra=[("dve", max(scol))])
                stm_done = p.cnt["dve"]
                omx = []
                for jj in range(NJ):
                    n1 = p.emit("vector",
                                lambda e, jj=jj: e.tensor_tensor(
                                    k_n[:, jj, :], v_b[:, jj, :], gate_tm[:, jj, :], ALU.mult),
                                waits=[("dve", stm_done)], inc="dve", tag=f"om1 {jj}")
                    omx.append(p.emit("vector",
                                      lambda e, jj=jj: e.tensor_tensor(
                                          kv_tm[:, jj, :], k_n[:, jj, :], q_n[:, jj, :], ALU.mult),
                                      waits=[("dve", n1), ("dve", q_done)], inc="dve", tag=f"om2 {jj}"))
                dbg_dump("omix", poolA[:, 16384:24576], "dve", it)
                transpose_set(lambda c, jj: kv_tm[:, jj, c * 128:(c + 1) * 128],
                              lambda c, jj: omix_fm[:, c, jj * 128:(jj + 1) * 128],
                              H, NJ, "om", waits_extra=[("dve", max(omx))])
                omixfm_done = p.cnt["dve"]

                gate()
                ybf = {}
                def y_out(jj, s, bank, nmm):
                    nv = p.emit("vector",
                                lambda e, jj=jj, s=s, bank=bank: e.tensor_copy(
                                    y_bf[:, jj, s * 512:(s + 1) * 512], bank[:]),
                                waits=[("pe", nmm)], inc="dve", tag=f"y ev {jj}.{s}")
                    ybf[(jj, s)] = nv
                    bfree(bank, dve=nv)
                proj_tm("wo1", range(4), omix_fm, 0, y_out, pass0, "wo1", [("dve", omixfm_done)])
                prev_m3 = 0
                for jj in range(NJ):
                    w0 = max(ybf[(jj, s)] for s in range(4))
                    na = p.emit("scalar",
                                lambda e, jj=jj: e.activation(junkD(jj * 2048, 2048), y_bf[:, jj, :],
                                                              AF.Square, accum_out=stat[:, jj:jj + 1]),
                                waits=[("dve", w0)], inc="act", tag=f"y sq {jj}")
                    ns = p.emit("vector", lambda e, jj=jj: e.reduce_sum(stat[:, 8 + jj:9 + jj],
                                                                        y_bf[:, jj, :], axis=AX.X),
                                waits=[("dve", w0)], inc="dve", tag=f"y sum {jj}")
                    nv1 = p.emit("vector", lambda e, jj=jj: e.tensor_scalar(
                        stat[:, 16 + jj:17 + jj], stat[:, 8 + jj:9 + jj], 1.0 / D, None, ALU.mult),
                                 waits=[("dve", ns)], inc="dve", tag="y mean")
                    nv2 = p.emit("vector", lambda e, jj=jj: e.tensor_tensor(
                        stat[:, 24 + jj:25 + jj], stat[:, 16 + jj:17 + jj],
                        stat[:, 16 + jj:17 + jj], ALU.mult),
                                 waits=[("dve", nv1)], inc="dve", tag="y m2")
                    nv3 = p.emit("vector", lambda e, jj=jj: e.scalar_tensor_tensor(
                        stat[:, 32 + jj:33 + jj], stat[:, jj:jj + 1], 1.0 / D,
                        stat[:, 24 + jj:25 + jj], ALU.mult, ALU.subtract),
                                 waits=[("act", na), ("dve", nv2)], inc="dve", tag="y var")
                    nv4 = p.emit("vector", lambda e, jj=jj: e.tensor_scalar(
                        stat[:, 40 + jj:41 + jj], stat[:, 32 + jj:33 + jj], 1e-5, None, ALU.add),
                                 waits=[("dve", nv3)], inc="dve", tag="y eps")
                    nv5 = p.emit("vector", lambda e, jj=jj: e.reciprocal(
                        stat[:, 48 + jj:49 + jj], stat[:, 40 + jj:41 + jj]),
                                 waits=[("dve", nv4)], inc="dve", tag="y recip")
                    na2 = p.emit("scalar", lambda e, jj=jj: e.activation(
                        stat[:, 56 + jj:57 + jj], stat[:, 48 + jj:49 + jj], AF.Sqrt),
                                 waits=[("dve", nv5)], inc="act", tag="y sqrt")
                    for q4 in range(4):
                        o = q4 * 512
                        m1 = p.emit("vector",
                                    lambda e, jj=jj, o=o: e.tensor_scalar(
                                        scrA[:], y_bf[:, jj, o:o + 512],
                                        stat[:, 16 + jj:17 + jj], stat[:, 56 + jj:57 + jj],
                                        ALU.subtract, ALU.mult),
                                    waits=[("act", na2), ("dve", prev_m3)], inc="dve", tag=f"y ap1 {jj}.{q4}")
                        m2 = p.emit("vector",
                                    lambda e, o=o: e.scalar_tensor_tensor(
                                        scrA[:], lnw_tbl[:, o:o + 512], 1.0, scrA[:], ALU.bypass, ALU.mult),
                                    waits=[("dve", m1)], inc="dve", tag=f"y ap2 {jj}.{q4}")
                        prev_m3 = p.emit("vector",
                               lambda e, jj=jj, o=o: e.scalar_tensor_tensor(
                                   s_tm[:, jj, o:o + 512], lnb_tbl[:, o:o + 512], 1.0, scrA[:],
                                   ALU.bypass, ALU.add),
                               waits=[("dve", m2)], inc="dve", tag=f"y ap3 {jj}.{q4}")
                s1_done = p.cnt["dve"]
                dbg_dump("s1", poolD[:], "dve", it)

                gate()
                prev_nh = 0
                for cs in range(32):
                    view, wws, slotid, extra = wm.load("wgu1", cs, pass0)
                    last = 0
                    for k in range(2):
                        c1x = 2 * cs + k
                        bg, bu = pb[2 * (k % 2)], pb[2 * (k % 2) + 1]
                        def gmm(e, view=view, k=k, bg=bg):
                            r = None
                            for fi in range(H):
                                r = e.matmul(bg[:], view[:, fi, k * 128:(k + 1) * 128], xn[:, fi, 3:515],
                                             start=(fi == 0), stop=(fi == H - 1))
                            return r
                        ng = p.emit("tensor", gmm,
                                    waits=wws + [("dve", xn_done)] + list(bank_free[id(bg)].items()),
                                    inc="pe", tag=f"f1g {c1x}")
                        def umm(e, view=view, k=k, bu=bu):
                            r = None
                            for fi in range(H):
                                r = e.matmul(bu[:], view[:, fi, 256 + k * 128:256 + (k + 1) * 128],
                                             xn[:, fi, 3:515], start=(fi == 0), stop=(fi == H - 1))
                            return r
                        nu = p.emit("tensor", umm,
                                    waits=wws + list(bank_free[id(bu)].items()),
                                    inc="pe", tag=f"f1u {c1x}")
                        last = nu
                        na = p.emit("scalar", lambda e, bg=bg: e.activation(sil_scr[:], bg[:], AF.Silu),
                                    waits=[("pe", ng), ("dve", prev_nh)], inc="act", tag=f"f1sil {c1x}")
                        prev_nh = p.emit("vector",
                                         lambda e, c1x=c1x, bu=bu: e.scalar_tensor_tensor(
                                             hdn[:, c1x, :], sil_scr[:], 1.0, bu[:], ALU.bypass, ALU.mult),
                                         waits=[("act", na), ("pe", nu)], inc="dve", tag=f"f1hdn {c1x}")
                        bfree(bg, act=na)
                        bfree(bu, dve=prev_nh)
                    wm.release(slotid, [("pe", last)], extra)
                hdn1_all = p.cnt["dve"]

                gate()
                g1_last = 0
                for s in range(16):
                    view, wws, slotid, extra = wm.load("wd1", s, pass0)
                    bank = pb[s % 2]
                    def dmm(e, view=view, bank=bank):
                        r = None
                        for c in range(I1C):
                            r = e.matmul(bank[:], view[:, c, :], hdn[:, c, :],
                                         start=(c == 0), stop=(c == I1C - 1))
                        return r
                    nmm = p.emit("tensor", dmm,
                                 waits=wws + [("dve", hdn1_all)] + list(bank_free[id(bank)].items()),
                                 inc="pe", tag=f"d1 {s}")
                    g1_last = p.emit("scalar",
                                     lambda e, s=s, bank=bank: e.activation(g_fm[:, s, :], bank[:], AF.Silu),
                                     waits=[("pe", nmm)], inc="act", tag=f"d1sil {s}")
                    bfree(bank, act=g1_last)
                    wm.release(slotid, [("pe", nmm)], extra)
                dbg_dump("g1", poolE[:], "act", it)

                gate()
                resid_stage(it, "x1", "out" if mode != "l0" else "x1", g1_last, s1_done)

        try:
            _tile_loop_body()
        except _StopEmit:
            pass

        fin = p.snap()

        def final_waits(e):
            e.wait_ge(sems["da"], fin["da"])
            e.wait_ge(sems["ds"], fin["ds"])
            return e.wait_ge(sems["dg"], fin["dg"])
        p.emit("sync", final_waits, tag="final")

        p.check()

        with nc.Block() as block:
            def runner(ops):
                def f(eng):
                    for op in ops:
                        op[0](eng)
                return f
            block.tensor(runner(p.q["tensor"]))
            block.vector(runner(p.q["vector"]))
            block.scalar(runner(p.q["scalar"]))
            block.gpsimd(runner(p.q["gpsimd"]))
            block.sync(runner(p.q["sync"]))

    return nc


def kernel(**inputs):
    W, C = _host_prepare(inputs)
    x = np.ascontiguousarray(np.asarray(inputs["L0_attn_x"], np.float32))

    nc = build_program(MODE, DEBUG)
    in_maps = []
    for core in range(NCORE):
        m = {"x": x[core]}
        m.update(W)
        m.update(C)
        in_maps.append(m)
    res = run_bass_kernel_spmd(nc, in_maps, core_ids=list(range(NCORE)))
    return np.stack([np.asarray(res.results[i]["out"]) for i in range(NCORE)], axis=0)


# version 02
# speedup vs baseline: 1.3659x; 1.3659x over previous
import contextlib
import numpy as np
import concourse.bass as bass
import concourse.mybir as mybir
from concourse.bass_utils import run_bass_kernel_spmd

F32 = mybir.dt.float32
BF16 = mybir.dt.bfloat16
AF = mybir.ActivationFunctionType
ALU = mybir.AluOpType
AX = mybir.AxisListType

D = 2048
T = 4096
MT = 512
NTILE = T // MT
NJ = MT // 128
H = 16
NG = 4
I0C = 44
I1C = 64
NCORE = 8

MODE = "full"
DEBUG = False
KTILES = NTILE
KSTAGES = 10**9
KLNPARTS = 99
STRICT_TR = True
KT2PARTS = 99

ENGINES = ("tensor", "vector", "scalar", "gpsimd", "sync")
DMA_SEMS = {"gpsimd": "dg", "sync": "ds", "scalar": "da"}


class Prog:
    def __init__(self, nc):
        self.nc = nc
        self.q = {e: [] for e in ENGINES}
        self.cnt = {s: 0 for s in ("pe", "dve", "act", "pool", "dg", "ds", "da")}
        self.sem = {}
        self.pending = {e: [] for e in ENGINES}
        self.enabled = True

    def emit(self, eng, fn, waits=(), inc=None, inc_n=1, tag=""):
        if not self.enabled:
            return self.cnt[inc] if inc is not None else None
        w = {}
        for s, c in list(self.pending[eng]) + list(waits):
            if c > 0:
                w[s] = max(w.get(s, 0), c)
        self.pending[eng] = []
        w = tuple(w.items())
        sem = self.sem
        def op(e, _fn=fn, _w=w, _inc=inc, _n=inc_n):
            for s, c in _w:
                e.wait_ge(sem[s], c)
            r = _fn(e)
            if _inc is not None and r is not None:
                r.then_inc(sem[_inc], _n)
        self.q[eng].append((op, w, inc, inc_n, tag))
        if inc is not None:
            self.cnt[inc] += inc_n
            return self.cnt[inc]
        return None

    def dma(self, eng, out_ap, in_ap, waits=(), tag=""):
        s = DMA_SEMS[eng]
        return self.emit(eng, lambda e, o=out_ap, i=in_ap: e.dma_start(out=o, in_=i),
                         waits=list(waits) + [(s, self.cnt[s])], inc=s, inc_n=16, tag=tag)

    def snap(self):
        return dict(self.cnt)

    def barrier(self, snap=None):
        c = snap or self.snap()
        for e in ENGINES:
            self.pending[e] = [(s, v) for s, v in c.items() if v > 0]
        return c

    def check(self):
        idx = {e: 0 for e in ENGINES}
        cur = {s: 0 for s in self.cnt}
        progress = True
        while progress:
            progress = False
            for e in ENGINES:
                while idx[e] < len(self.q[e]):
                    op, w, inc, inc_n, tag = self.q[e][idx[e]]
                    if any(cur[s] < c for s, c in w):
                        break
                    if inc is not None:
                        cur[inc] += inc_n
                    idx[e] += 1
                    progress = True
        stuck = {e: (idx[e], len(self.q[e])) for e in ENGINES if idx[e] < len(self.q[e])}
        if stuck:
            det = []
            for e, (i, n) in stuck.items():
                op, w, inc, inc_n, tag = self.q[e][i]
                miss = [(s, c, cur[s]) for s, c in w if cur[s] < c]
                det.append(f"{e}[{i}/{n}] tag={tag} missing={miss}")
            raise RuntimeError("build-time deadlock: " + "; ".join(det))
        for e in ENGINES:
            for op, w, inc, inc_n, tag in self.q[e]:
                for s, c in w:
                    if c > self.cnt[s]:
                        raise RuntimeError(f"wait beyond final: {e} tag={tag} {s}>={c} final={self.cnt[s]}")


def _tile_w(w, fo_slab):
    FO, FI = w.shape
    ns, c = FO // fo_slab, FI // 128
    a = np.asarray(w, dtype=np.float32).reshape(ns, fo_slab, c, 128)
    return np.ascontiguousarray(a.transpose(0, 3, 2, 1))


def _host_prepare(inp):
    W = {}
    def _half_fi(a):
        ns, p, c, f = a.shape
        return np.ascontiguousarray(a.reshape(ns, p, 2, c // 2, f).transpose(0, 2, 1, 3, 4)
                                    ).reshape(2 * ns, p, c // 2, f)
    W["wqkvu"] = _tile_w(np.asarray(inp["L0_attn_W_qkvu"]), 256)
    W["wo0"] = _tile_w(np.asarray(inp["L0_attn_W_out"]), 256)
    g0 = _tile_w(np.asarray(inp["L0_ffn_gate_w"]), 128)
    u0 = _tile_w(np.asarray(inp["L0_ffn_up_w"]), 128)
    W["wgu0"] = np.ascontiguousarray(np.concatenate([g0, u0], axis=3))
    W["wd0"] = _half_fi(_tile_w(np.asarray(inp["L0_ffn_down_w"]), 128))
    W["wq1"] = _tile_w(np.asarray(inp["L1_attn_Wq"]), 256)
    W["wk1"] = _tile_w(np.asarray(inp["L1_attn_Wk"]), 256)
    W["wv1"] = _tile_w(np.asarray(inp["L1_attn_Wv"]), 256)
    W["wgt1"] = _tile_w(np.asarray(inp["L1_attn_gate_w"]), 256)
    W["wo1"] = _tile_w(np.asarray(inp["L1_attn_Wo"]), 256)
    g1 = _tile_w(np.asarray(inp["L1_ffn_w_gate"]), 128)
    u1 = _tile_w(np.asarray(inp["L1_ffn_w_up"]), 128)
    W["wgu1"] = np.ascontiguousarray(np.concatenate([g1, u1], axis=3))
    W["wd1"] = _half_fi(_tile_w(np.asarray(inp["L1_ffn_w_down"]), 128))

    C = {}
    i = np.arange(128, dtype=np.float32)
    dd = i[None, :] - i[:, None]
    C["c_ddiag"] = np.where(dd >= 0, dd, 3.0e4).astype(np.float32)
    C["c_dqfull"] = np.broadcast_to(i[None, :] + 1.0, (128, 128)).astype(np.float32).copy()
    C["c_drev"] = (127.0 - i)[:, None].astype(np.float32).copy()
    C["c_eye"] = np.eye(128, dtype=np.float32)
    C["c_slope"] = np.asarray(inp["L0_attn_slope_rate"], np.float32).reshape(1, H).copy()
    C["c_gamma"] = np.asarray(inp["L1_attn_gamma"], np.float32).reshape(1, H).copy()
    C["c_gbrow"] = np.asarray(inp["L1_attn_gate_b"], np.float32).reshape(1, D).copy()
    C["c_lnwrow"] = np.asarray(inp["L1_attn_ln_w"], np.float32).reshape(1, D).copy()
    C["c_lnbrow"] = np.asarray(inp["L1_attn_ln_b"], np.float32).reshape(1, D).copy()
    cw = np.asarray(inp["L1_attn_conv_w"], np.float32).reshape(H, 128, 4)
    C["c_cw"] = np.ascontiguousarray(cw.transpose(1, 0, 2)).reshape(128, H * 4)
    cb = np.asarray(inp["L1_attn_conv_b"], np.float32).reshape(H, 128)
    C["c_cb"] = np.ascontiguousarray(cb.T)
    C["c_gs"] = np.ascontiguousarray(np.asarray(inp["L0_ffn_gate_s"], np.float32).reshape(I0C, 128).T)
    C["c_us"] = np.ascontiguousarray(np.asarray(inp["L0_ffn_up_s"], np.float32).reshape(I0C, 128).T)
    C["c_ds"] = np.ascontiguousarray(np.asarray(inp["L0_ffn_down_s"], np.float32).reshape(16, 128).T)
    return W, C


SLAB_SHAPES = {
    "wqkvu": (32, 16, 256), "wo0": (8, 16, 256), "wgu0": (44, 16, 256),
    "wd0": (32, 22, 128), "wq1": (8, 16, 256),
    "wk1": (8, 16, 256), "wv1": (8, 16, 256), "wgt1": (8, 16, 256),
    "wo1": (8, 16, 256), "wgu1": (64, 16, 256),
    "wd1": (32, 32, 128),
}
L0_W = ("wqkvu", "wo0", "wgu0", "wd0")


class WeightMgr:
    def __init__(self, p, wf32, wbf16, wsl):
        self.p = p
        self.wf32 = wf32
        self.wbf16 = wbf16
        self.wsl = wsl
        self.slot_free = [dict() for _ in wsl]
        self.turn = 0
        self.written_back = set()

    def load(self, name, s, pass0):
        p = self.p
        slot = self.turn
        self.turn = (self.turn + 1) % len(self.wsl)
        nsl, c, cols = SLAB_SHAPES[name]
        view = self.wsl[slot][:, 0:c * cols].rearrange("p (c f) -> p c f", c=c, f=cols)
        waits = list(self.slot_free[slot].items())
        key = (name, s)
        if pass0 and key not in self.written_back:
            n = p.dma("gpsimd", view, self.wf32[name][s], waits=waits, tag=f"ld0 {name}{s}")
            wb = p.dma("sync", self.wbf16[name][s], view, waits=[("dg", n)], tag=f"wb {name}{s}")
            self.written_back.add(key)
            return view, [("dg", n)], slot, [("ds", wb)]
        n = p.dma("sync", view, self.wbf16[name][s], waits=waits, tag=f"ld {name}{s}")
        return view, [("ds", n)], slot, []

    def release(self, slot, waits, extra=()):
        d = {}
        for s, c in list(waits) + list(extra):
            d[s] = max(d.get(s, 0), c)
        self.slot_free[slot] = d


def build_program(mode, debug=False):
    nc = bass.Bass("TRN2", target_bir_lowering=False)
    layers = {"l0": (0,), "l1": (1,), "full": (0, 1)}[mode]

    x_in = nc.dram_tensor("x", [T, D], F32, kind="ExternalInput").ap() if 0 in layers else None
    if mode == "full":
        x1 = nc.dram_tensor("x1buf", [T, D], F32, kind="Internal").ap()
        out = nc.dram_tensor("out", [T, D], F32, kind="ExternalOutput").ap()
    elif mode == "l0":
        x1 = nc.dram_tensor("out", [T, D], F32, kind="ExternalOutput").ap()
        out = None
    else:
        x1 = nc.dram_tensor("x1", [T, D], F32, kind="ExternalInput").ap()
        out = nc.dram_tensor("out", [T, D], F32, kind="ExternalOutput").ap()

    wf32, wbf16 = {}, {}
    for name, (nsl, c, cols) in SLAB_SHAPES.items():
        if ((0 in layers) if name in L0_W else (1 in layers)):
            wf32[name] = nc.dram_tensor(name, [nsl, 128, c, cols], F32, kind="ExternalInput").ap()
            wbf16[name] = nc.dram_tensor(name + "_bf", [nsl, 128, c, cols], BF16, kind="Internal").ap()

    cN = {"c_ddiag": [128, 128], "c_dqfull": [128, 128], "c_drev": [128, 1],
          "c_eye": [128, 128], "c_slope": [1, H], "c_gamma": [1, H],
          "c_gbrow": [1, D], "c_lnwrow": [1, D], "c_lnbrow": [1, D],
          "c_cw": [128, H * 4], "c_cb": [128, H], "c_gs": [128, I0C],
          "c_us": [128, I0C], "c_ds": [128, 16]}
    cin = {k: nc.dram_tensor(k, v, F32, kind="ExternalInput").ap() for k, v in cN.items()}

    dbg = {}
    def dbg_def(name, shape, dt=BF16):
        if debug:
            dbg[name] = nc.dram_tensor("dbg_" + name, [NTILE] + shape, dt, kind="ExternalOutput").ap()
    dbg_def("xn0", [128, H * 515]); dbg_def("s0", [128, NJ * D]); dbg_def("g0", [128, 16 * 512])
    dbg_def("o2", [128, NJ * D]); dbg_def("xn1", [128, H * 515]); dbg_def("xconv", [128, H * 512])
    dbg_def("kvtm", [128, NJ * D]); dbg_def("sfm", [128, H * 512]); dbg_def("omix", [128, NJ * D])
    dbg_def("s1", [128, NJ * D]); dbg_def("g1", [128, 16 * 512])

    stack = contextlib.ExitStack()
    sbuf = lambda name, shape, dt: stack.enter_context(nc.sbuf_tensor(name, shape, dt))
    psum = lambda name, shape, dt: stack.enter_context(nc.psum_tensor(name, shape, dt))

    with stack:
        xst0 = sbuf("xst0", [128, D], F32)
        xn = sbuf("xn", [128, H, 515], BF16)
        halo = sbuf("halo", [128, H, 3], BF16)
        poolA = sbuf("poolA", [128, 32768], BF16)
        poolB = sbuf("poolB", [128, 10240], BF16)
        poolC = sbuf("poolC", [128, 8192], BF16)
        poolD = sbuf("poolD", [128, 8192], BF16)
        poolE = sbuf("poolE", [128, 8192], BF16)
        kv_bf = sbuf("kv_bf", [128, D], BF16)
        s_state = sbuf("s_state", [128, H], F32)
        wsl = [sbuf("wsl0", [128, 4096], BF16), sbuf("wsl1", [128, 4096], BF16)]
        id_bf = sbuf("id_bf", [128, 128], BF16)
        masks = sbuf("masks", [128, H, 128], BF16)
        qdt = sbuf("qdt", [128, H * 128], BF16)
        rbc = sbuf("rbc", [128, H], F32)
        gbc = sbuf("gbc", [128, H], F32)
        bdc = sbuf("bdc", [128, H], F32)
        kdc = sbuf("kdc", [128, H], F32)
        cw = sbuf("cw", [128, H, 4], F32)
        cb = sbuf("cb", [128, H], F32)
        gsc = sbuf("gsc", [128, I0C], F32)
        usc = sbuf("usc", [128, I0C], F32)
        dsc = sbuf("dsc", [128, 16], F32)
        gb_tbl = sbuf("gb_tbl", [128, D], BF16)
        lnw_tbl = sbuf("lnw_tbl", [128, D], BF16)
        lnb_tbl = sbuf("lnb_tbl", [128, D], BF16)
        scrA = sbuf("scrA", [128, 512], F32)
        scrB = sbuf("scrB", [128, 512], F32)
        scrC = sbuf("scrC", [128, 512], F32)
        sil_scr = sbuf("sil_scr", [128, 512], BF16)
        hstage = sbuf("hstage", [128, 512], BF16)
        stat = sbuf("stat", [128, 64], F32)
        nrm = sbuf("nrm", [128, NJ * H], F32)
        ssc = sbuf("ssc", [128, NJ, H], F32)
        onesf = scrB[0:1, 128:256]
        rowbuf = scrC[0:1, 0:512]
        dd_sb = scrA[:, 0:128]
        dq_sb = scrA[:, 128:256]
        drev_sb = scrA[:, 256:257]
        tmp128 = scrB[:, 0:128]
        ktk = [sil_scr[:, 256:384], sil_scr[:, 384:512]]
        qkm = [sil_scr[:, 0:128], sil_scr[:, 128:256]]
        sgt = scrC[:, 0:128]

        pb = [psum(f"pb{i}", [128, 512], F32) for i in range(4)]
        pqk_b = psum("pqk", [128, 512], F32)
        po_b = psum("po", [128, 512], F32)
        pkv_b = psum("pkv", [128, 512], F32)
        ptr = psum("ptr", [128, 1024], BF16)
        pqk = pqk_b[:, 0:128]
        po = po_b[:, 0:128]
        pkv = pkv_b[:, 0:128]
        pt_slot = [ptr[:, 0:128], ptr[:, 512:640]]

        sems = {}
        for s in ("pe", "dve", "act", "pool", "dg", "ds", "da"):
            sems[s] = stack.enter_context(nc.semaphore("sem_" + s))

        p = Prog(nc)
        p.sem = sems
        wm = WeightMgr(p, wf32, wbf16, wsl)
        tr_ev = [0, 0]
        tr_last = [0]

        vA = poolA[:]
        hdn = vA.rearrange("p (c f) -> p c f", c=64, f=512)
        slot = [vA[:, i * 8192:(i + 1) * 8192].rearrange("p (j f) -> p j f", j=NJ, f=D)
                for i in range(4)]
        grp = [poolB[:, i * 2048:(i + 1) * 2048].rearrange("p (j f) -> p j f", j=NJ, f=512)
               for i in range(5)]
        gate_tm = poolB[:, 0:8192].rearrange("p (j f) -> p j f", j=NJ, f=D)
        o2_tm = poolC[:].rearrange("p (j f) -> p j f", j=NJ, f=D)
        xconv = poolC[:].rearrange("p (c f) -> p c f", c=H, f=512)
        s_fm = poolC[:].rearrange("p (c f) -> p c f", c=H, f=512)
        y_bf = poolC[:].rearrange("p (j f) -> p j f", j=NJ, f=D)
        s_tm = poolD[:].rearrange("p (j f) -> p j f", j=NJ, f=D)
        g_fm = poolE[:].rearrange("p (c f) -> p c f", c=H, f=512)
        o2fm = g_fm
        omix_fm = g_fm
        xv = {}
        if x_in is not None:
            xv["x"] = x_in.rearrange("(n p) f -> n p f", p=128)
        xv["x1"] = x1.rearrange("(n p) f -> n p f", p=128)
        if out is not None:
            xv["out"] = out.rearrange("(n p) f -> n p f", p=128)

        def qd_rep(h):
            return bass.AP(qdt, h * 128, [[H * 128, 128], [0, NJ], [1, 128]])

        def junkD(off, width):
            return bass.AP(poolD, off, [[8192, 128], [1, width]])

        def dbg_dump(name, src_ap, wait_sem, it):
            if debug and name in dbg:
                p.emit("scalar", lambda e, it=it, s=src_ap: e.dma_start(out=dbg[name][it], in_=s),
                       waits=[(wait_sem, p.cnt[wait_sem]), ("da", p.cnt["da"])],
                       inc="da", inc_n=16, tag="dbg " + name)
                p.barrier()

        class _StopEmit(Exception):
            pass
        stage_i = [0]

        def gate():
            stage_i[0] += 1
            if stage_i[0] > KSTAGES:
                p.enabled = False

        for k, ap in cin.items():
            tgt = {"c_ddiag": dd_sb, "c_dqfull": dq_sb, "c_drev": drev_sb,
                   "c_cw": cw[:], "c_cb": cb[:], "c_gs": gsc[:], "c_us": usc[:], "c_ds": dsc[:]}.get(k)
            if k == "c_eye":
                p.dma("gpsimd", id_bf[:], ap, tag="eye")
            elif tgt is not None:
                p.dma("gpsimd", tgt, ap, tag=k)
        dg_consts = p.cnt["dg"]

        for buf in (kv_bf, s_state, halo, xn):
            p.emit("gpsimd", lambda e, b=buf: e.memset(b[:], 0.0), inc="pool", tag="memset")
        p.emit("gpsimd", lambda e: e.memset(onesf, 1.0), inc="pool", tag="ones")
        pool_init = p.cnt["pool"]

        gate()

        def bcast_row(src_ap, width, dst, tag):
            for b0 in range(0, width, 512):
                w = min(512, width - b0)
                nd = p.dma("gpsimd", rowbuf[0:1, 0:w], src_ap[0:1, b0:b0 + w],
                           waits=[("pe", p.cnt["pe"])], tag="row " + tag)
                nmm = p.emit("tensor",
                             lambda e, w=w: e.matmul(pb[0][:, 0:w], onesf[:],
                                                     rowbuf[:, 0:w], start=True, stop=True),
                             waits=[("dg", nd), ("pool", pool_init), ("dve", p.cnt["dve"])],
                             inc="pe", tag="bc mm " + tag)
                p.emit("vector",
                       lambda e, b0=b0, w=w, d=dst: e.tensor_copy(d[:, b0:b0 + w], pb[0][:, 0:w]),
                       waits=[("pe", nmm)], inc="dve", tag="bc ev " + tag)

        bcast_row(cin["c_slope"], H, rbc, "slope")
        gate()
        if 1 in layers:
            bcast_row(cin["c_gamma"], H, gbc, "gamma")
            bcast_row(cin["c_gbrow"], D, gb_tbl, "gb")
            bcast_row(cin["c_lnwrow"], D, lnw_tbl, "lnw")
            bcast_row(cin["c_lnbrow"], D, lnb_tbl, "lnb")

        gate()
        if 0 in layers:
            c0 = p.snap()
            n1 = p.emit("vector", lambda e: e.tensor_scalar(stat[:, 0:H], rbc[:], -128.0, None, ALU.mult),
                        waits=[("dve", c0["dve"])], inc="dve", tag="bd pre")
            p.emit("scalar", lambda e: e.activation(bdc[:], stat[:, 0:H], AF.Exp),
                   waits=[("dve", n1)], inc="act", tag="bd exp")
            drev_b = bass.AP(scrA, 256, [[512, 128], [0, H]])
            n2 = p.emit("vector", lambda e: e.tensor_tensor(stat[:, H:2 * H], rbc[:], drev_b, ALU.mult),
                        waits=[("dve", c0["dve"]), ("dg", dg_consts)], inc="dve", tag="kd pre")
            p.emit("scalar", lambda e: e.activation(kdc[:], stat[:, H:2 * H], AF.Exp, scale=-1.0),
                   waits=[("dve", n2)], inc="act", tag="kd exp")
            for h in range(H):
                nm = p.emit("vector",
                            lambda e, h=h: e.tensor_scalar(tmp128, dd_sb, rbc[:, h:h + 1], None, ALU.mult),
                            waits=[("dg", dg_consts), ("act", p.cnt["act"])], inc="dve", tag=f"mpre {h}")
                p.emit("scalar", lambda e, h=h: e.activation(masks[:, h, :], tmp128, AF.Exp, scale=-1.0),
                       waits=[("dve", nm)], inc="act", tag=f"mexp {h}")
                nq = p.emit("vector",
                            lambda e, h=h: e.tensor_scalar(tmp128, dq_sb, rbc[:, h:h + 1], None, ALU.mult),
                            waits=[("act", p.cnt["act"])], inc="dve", tag=f"qpre {h}")
                p.emit("scalar", lambda e, h=h: e.activation(
                    bass.AP(qdt, h * 128, [[H * 128, 128], [1, 128]]), tmp128, AF.Exp, scale=-1.0),
                       waits=[("dve", nq)], inc="act", tag=f"qexp {h}")

        gate()
        p.barrier()

        bank_free = {id(b): {} for b in pb}

        def bfree(bank, **kw):
            d = bank_free[id(bank)]
            for k, v in kw.items():
                d[k] = max(d.get(k, 0), v)

        def ln_stage(it, src):
            en = p.enabled
            for jj in range(NJ):
                c0 = p.snap()
                nl = p.dma("gpsimd", xst0[:], xv[src][it * NJ + jj],
                           waits=[("dve", c0["dve"]), ("da", c0["da"])],
                           tag=f"ln ld {jj}")
                if KLNPARTS < 2:
                    p.enabled = False
                na = p.emit("scalar",
                            lambda e, jj=jj: e.activation(junkD(jj * 2048, 2048), xst0[:], AF.Square,
                                                          accum_out=stat[:, jj:jj + 1]),
                            waits=[("dg", nl), ("dve", c0["dve"])], inc="act", tag=f"ln sq {jj}")
                ns = p.emit("vector", lambda e, jj=jj: e.reduce_sum(stat[:, 8 + jj:9 + jj], xst0[:], axis=AX.X),
                            waits=[("dg", nl)], inc="dve", tag=f"ln sum {jj}")
                if KLNPARTS < 3:
                    p.enabled = False
                nv1 = p.emit("vector", lambda e, jj=jj: e.tensor_scalar(
                    stat[:, 16 + jj:17 + jj], stat[:, 8 + jj:9 + jj], 1.0 / D, None, ALU.mult),
                             waits=[("dve", ns)], inc="dve", tag="ln mean")
                nv2 = p.emit("vector", lambda e, jj=jj: e.tensor_tensor(
                    stat[:, 24 + jj:25 + jj], stat[:, 16 + jj:17 + jj], stat[:, 16 + jj:17 + jj], ALU.mult),
                             waits=[("dve", nv1)], inc="dve", tag="ln m2")
                nv3 = p.emit("vector", lambda e, jj=jj: e.scalar_tensor_tensor(
                    stat[:, 32 + jj:33 + jj], stat[:, jj:jj + 1], 1.0 / D, stat[:, 24 + jj:25 + jj],
                    ALU.mult, ALU.subtract),
                             waits=[("act", na), ("dve", nv2)], inc="dve", tag="ln var")
                nv4 = p.emit("vector", lambda e, jj=jj: e.tensor_scalar(
                    stat[:, 40 + jj:41 + jj], stat[:, 32 + jj:33 + jj], 1e-5, None, ALU.add),
                             waits=[("dve", nv3)], inc="dve", tag="ln eps")
                nv5 = p.emit("vector", lambda e, jj=jj: e.reciprocal(stat[:, 48 + jj:49 + jj], stat[:, 40 + jj:41 + jj]),
                             waits=[("dve", nv4)], inc="dve", tag="ln recip")
                na2 = p.emit("scalar", lambda e, jj=jj: e.activation(
                    stat[:, 56 + jj:57 + jj], stat[:, 48 + jj:49 + jj], AF.Sqrt),
                             waits=[("dve", nv5)], inc="act", tag="ln sqrt")
                if KLNPARTS < 4:
                    p.enabled = False
                nap = p.emit("vector",
                             lambda e, jj=jj: e.tensor_scalar(s_tm[:, jj, :], xst0[:],
                                                              stat[:, 16 + jj:17 + jj], stat[:, 56 + jj:57 + jj],
                                                              ALU.subtract, ALU.mult),
                             waits=[("act", na2)], inc="dve", tag=f"ln apply {jj}")
                if KLNPARTS < 5:
                    p.enabled = False
                for c in range(H):
                    sl = (jj * H + c) % 2
                    ntr = p.emit("tensor",
                                 lambda e, jj=jj, c=c, sl=sl: e.transpose(
                                     pt_slot[sl], s_tm[:, jj, c * 128:(c + 1) * 128], id_bf[:]),
                                 waits=[("dve", nap), ("dve", tr_ev[1] if STRICT_TR else tr_ev[sl]),
                                        ("dve", tr_ev[0] if STRICT_TR else 0)],
                                 inc="pe", tag=f"lntr {jj}.{c}")
                    tr_ev[sl] = p.emit("vector",
                                       lambda e, jj=jj, c=c, sl=sl: e.tensor_copy(
                                           xn[:, c, 3 + jj * 128:3 + (jj + 1) * 128], pt_slot[sl]),
                                       waits=[("pe", ntr)], inc="dve", tag=f"lnev {jj}.{c}")
                p.enabled = en

        def transpose_set(get_src, get_dst, n_c, n_j, tag, waits_extra=()):
            for c in range(n_c):
                for jj in range(n_j):
                    sl = (c * n_j + jj) % 2
                    s_ap = get_src(c, jj)
                    d_ap = get_dst(c, jj)
                    ntr = p.emit("tensor",
                                 lambda e, s_ap=s_ap, sl=sl: e.transpose(pt_slot[sl], s_ap, id_bf[:]),
                                 waits=list(waits_extra) + [("dve", tr_ev[0]), ("dve", tr_ev[1])],
                                 inc="pe", tag=f"tr {tag} {c}.{jj}")
                    tr_ev[sl] = p.emit("vector",
                                       lambda e, d_ap=d_ap, sl=sl: e.tensor_copy(d_ap, pt_slot[sl]),
                                       waits=[("pe", ntr)], inc="dve", tag=f"trev {tag}")

        def proj_tm(w_name, s_range, lhs3, lhs_off, out_cb, pass0, tag, lhs_waits):
            for s in s_range:
                view, wws, slotid, extra = wm.load(w_name, s, pass0)
                last_pe = 0
                for jj in range(NJ):
                    bank = pb[jj % 2 + (2 if (s % 2) else 0)]
                    def mmf(e, view=view, jj=jj, bank=bank):
                        r = None
                        for fi in range(H):
                            r = e.matmul(bank[:, 0:256], lhs3[:, fi, lhs_off + jj * 128:lhs_off + (jj + 1) * 128],
                                         view[:, fi, :], start=(fi == 0), stop=(fi == H - 1))
                        return r
                    nmm = p.emit("tensor", mmf,
                                 waits=wws + list(lhs_waits) + list(bank_free[id(bank)].items()),
                                 inc="pe", tag=f"{tag} mm s{s} j{jj}")
                    last_pe = nmm
                    out_cb(jj, s, bank, nmm)
                wm.release(slotid, [("pe", last_pe)], extra)

        def resid_stage(it, src, dst, g_ready_act, s_ready_dve):
            nonlocal xst1_store
            prev_last = 0
            for jj in range(NJ):
                nl = p.dma("gpsimd", xst0[:], xv[src][it * NJ + jj],
                           waits=[("dve", p.cnt["dve"]), ("da", p.cnt["da"])], tag=f"res ld {jj}")
                chunk = [prev_last]
                for c in range(H):
                    sl = c % 2
                    ntr = p.emit("tensor",
                                 lambda e, c=c, jj=jj, sl=sl: e.transpose(
                                     pt_slot[sl], g_fm[:, c, jj * 128:(jj + 1) * 128], id_bf[:]),
                                 waits=[("act", g_ready_act), ("dve", tr_ev[0]), ("dve", tr_ev[1])],
                                 inc="pe", tag=f"gtr {jj}.{c}")
                    nsg = p.emit("vector",
                                 lambda e, c=c, jj=jj, sl=sl: e.tensor_tensor(
                                     sgt, pt_slot[sl], s_tm[:, jj, c * 128:(c + 1) * 128], ALU.mult),
                                 waits=[("pe", ntr), ("dve", s_ready_dve), ("dve", chunk[-1])],
                                 inc="dve", tag=f"sg {jj}.{c}")
                    tr_ev[sl] = nsg
                    nxo = p.emit("vector",
                                 lambda e, c=c: e.tensor_tensor(
                                     xst0[:, c * 128:(c + 1) * 128], xst0[:, c * 128:(c + 1) * 128],
                                     sgt, ALU.add),
                                 waits=[("dve", nsg), ("dg", nl)],
                                 inc="dve", tag=f"xo {jj}.{c}")
                    chunk.append(nxo)
                prev_last = chunk[-1]
                xst1_store = p.emit("scalar",
                                    lambda e, jj=jj, dst=dst, it=it: e.dma_start(out=xv[dst][it * NJ + jj], in_=xst0[:]),
                                    waits=[("dve", max(chunk)), ("da", p.cnt["da"])],
                                    inc="da", inc_n=16, tag=f"res st {jj}")

        xst1_store = 0

        kvbf_cnt = {h: 0 for h in range(H)}

        def _tile_loop_body():
          nonlocal xst1_store
          for it in range(KTILES):
            pass0 = (it == 0)
            if 0 in layers:
                gate()
                ln_stage(it, "x")
                xn_done = p.cnt["dve"]
                dbg_dump("xn0", xn[:].rearrange("p a b -> p (a b)"), "dve", it)

                gate()
                q_sil, qqd, k_sil, v_tm, u_tm = grp
                for g in range(NG):
                    def proj_fm(w_name, slabs, out_cb, tag):
                        last = 0
                        for si, s in enumerate(slabs):
                            view, wws, slotid, extra = wm.load(w_name, s, pass0)
                            for j in range(2):
                                jloc = 2 * si + j
                                bank = pb[jloc % 2]
                                def mmf(e, view=view, j=j, bank=bank):
                                    r = None
                                    for fi in range(H):
                                        r = e.matmul(bank[:], view[:, fi, j * 128:(j + 1) * 128],
                                                     xn[:, fi, 3:515], start=(fi == 0), stop=(fi == H - 1))
                                    return r
                                nmm = p.emit("tensor", mmf,
                                             waits=wws + [("dve", xn_done)] + list(bank_free[id(bank)].items()),
                                             inc="pe", tag=f"{tag} mm {jloc}")
                                last = nmm
                                out_cb(jloc, bank, nmm)
                            wm.release(slotid, [("pe", last)], extra)

                    def q_out(jloc, bank, nmm, g=g):
                        h = 4 * g + jloc
                        na = p.emit("scalar",
                                    lambda e, jloc=jloc, bank=bank: e.activation(q_sil[:, jloc, :], bank[:], AF.Silu),
                                    waits=[("pe", nmm)], inc="act", tag=f"q sil {h}")
                        p.emit("vector",
                               lambda e, jloc=jloc, h=h: e.tensor_tensor(
                                   qqd[:, jloc, :].rearrange("p (r f) -> p r f", r=NJ),
                                   q_sil[:, jloc, :].rearrange("p (r f) -> p r f", r=NJ),
                                   qd_rep(h), ALU.mult),
                               waits=[("act", na)], inc="dve", tag=f"qqd {h}")
                        bfree(bank, act=na)
                    proj_fm("wqkvu", [2 * g, 2 * g + 1], q_out, f"q g{g}")

                    def k_out(jloc, bank, nmm, g=g):
                        na = p.emit("scalar",
                                    lambda e, jloc=jloc, bank=bank: e.activation(k_sil[:, jloc, :], bank[:], AF.Silu),
                                    waits=[("pe", nmm)], inc="act", tag=f"k sil g{g}.{jloc}")
                        bfree(bank, act=na)
                    proj_fm("wqkvu", [8 + 2 * g, 9 + 2 * g], k_out, f"k g{g}")

                    def v_out(jj, s, bank, nmm, g=g):
                        b0 = (s - 16 - 2 * g) * 256
                        nv = p.emit("vector",
                                    lambda e, jj=jj, b0=b0, bank=bank: e.tensor_copy(
                                        v_tm[:, jj, b0:b0 + 256], bank[:, 0:256]),
                                    waits=[("pe", nmm)], inc="dve", tag=f"v ev {jj}")
                        bfree(bank, dve=nv)
                    proj_tm("wqkvu", [16 + 2 * g, 17 + 2 * g], xn, 3, v_out, pass0, f"v g{g}", [("dve", xn_done)])

                    def u_out(jj, s, bank, nmm, g=g):
                        b0 = (s - 24 - 2 * g) * 256
                        nv = p.emit("vector",
                                    lambda e, jj=jj, b0=b0, bank=bank: e.tensor_copy(
                                        u_tm[:, jj, b0:b0 + 256], bank[:, 0:256]),
                                    waits=[("pe", nmm)], inc="dve", tag=f"u ev {jj}")
                        bfree(bank, dve=nv)
                    proj_tm("wqkvu", [24 + 2 * g, 25 + 2 * g], xn, 3, u_out, pass0, f"u g{g}", [("dve", xn_done)])

                    grp_ready = p.snap()
                    po_free = {"act": 0, "dve": 0}
                    pqk_free = 0
                    pkv_free = 0
                    en2 = p.enabled
                    for blk in range(NJ):
                        for hl in range(4):
                            h = 4 * g + hl
                            sl = (blk * 4 + hl) % 2
                            if KT2PARTS < 2:
                                p.enabled = False
                            ntr = p.emit("tensor",
                                         lambda e, hl=hl, blk=blk, sl=sl: e.transpose(
                                             pt_slot[sl], k_sil[:, hl, blk * 128:(blk + 1) * 128], id_bf[:]),
                                         waits=[("act", grp_ready["act"]), ("dve", tr_ev[0]), ("dve", tr_ev[1])],
                                         inc="pe", tag=f"ktr {h}.{blk}")
                            nkd = p.emit("vector",
                                         lambda e, h=h, sl=sl: e.tensor_scalar(
                                             ktk[sl], pt_slot[sl], kdc[:, h:h + 1], None, ALU.mult),
                                         waits=[("pe", ntr)], inc="dve", tag=f"ktk {h}.{blk}")
                            tr_ev[sl] = nkd
                            if KT2PARTS < 3:
                                p.enabled = False
                            nqk = p.emit("tensor",
                                         lambda e, hl=hl, blk=blk: e.matmul(
                                             pqk, k_sil[:, hl, blk * 128:(blk + 1) * 128],
                                             q_sil[:, hl, blk * 128:(blk + 1) * 128], start=True, stop=True),
                                         waits=[("dve", pqk_free)], inc="pe", tag=f"qk {h}.{blk}")
                            nmsk = p.emit("vector",
                                          lambda e, h=h, sl=sl: e.scalar_tensor_tensor(
                                              qkm[sl], masks[:, h, :], 1.0, pqk, ALU.bypass, ALU.mult),
                                          waits=[("pe", nqk)], inc="dve", tag=f"mask {h}.{blk}")
                            pqk_free = nmsk
                            if KT2PARTS < 4:
                                p.enabled = False
                            def o_mm(e, hl=hl, blk=blk, h=h, sl=sl):
                                e.matmul(po, qqd[:, hl, blk * 128:(blk + 1) * 128],
                                         kv_bf[:, h * 128:(h + 1) * 128], start=True, stop=False)
                                return e.matmul(po, qkm[sl], v_tm[:, blk, hl * 128:(hl + 1) * 128],
                                                start=False, stop=True)
                            nomm = p.emit("tensor", o_mm,
                                          waits=[("dve", nmsk), ("dve", kvbf_cnt[h]),
                                                 ("act", po_free["act"]), ("dve", po_free["dve"])],
                                          inc="pe", tag=f"o {h}.{blk}")
                            if KT2PARTS < 5:
                                p.enabled = False
                            nkvm = p.emit("tensor",
                                          lambda e, blk=blk, hl=hl, sl=sl: e.matmul(
                                              pkv, ktk[sl], v_tm[:, blk, hl * 128:(hl + 1) * 128],
                                              start=True, stop=True),
                                          waits=[("dve", nkd), ("dve", pkv_free)], inc="pe", tag=f"kvmm {h}.{blk}")
                            if KT2PARTS < 6:
                                p.enabled = False
                            no2 = p.emit("vector",
                                         lambda e, blk=blk, hl=hl, h=h: e.scalar_tensor_tensor(
                                             o2_tm[:, blk, h * 128:(h + 1) * 128],
                                             u_tm[:, blk, hl * 128:(hl + 1) * 128], 1.0,
                                             po, ALU.bypass, ALU.mult),
                                         waits=[("pe", nomm)], inc="dve", tag=f"o2 {h}.{blk}")
                            nss = p.emit("scalar",
                                         lambda e, blk=blk, h=h: e.activation(
                                             junkD(blk * 2048 + ((h * 128) % 2048), 128),
                                             po, AF.Square, accum_out=ssc[:, blk, h:h + 1]),
                                         waits=[("pe", nomm), ("dve", no2)], inc="act", tag=f"ss {h}.{blk}")
                            po_free = {"act": nss, "dve": no2}
                            if KT2PARTS < 7:
                                p.enabled = False
                            nkv1 = p.emit("vector",
                                          lambda e, h=h: e.scalar_tensor_tensor(
                                              kv_bf[:, h * 128:(h + 1) * 128], kv_bf[:, h * 128:(h + 1) * 128],
                                              bdc[:, h:h + 1], pkv, ALU.mult, ALU.add),
                                          waits=[("pe", nkvm)], inc="dve", tag=f"kvup {h}.{blk}")
                            pkv_free = nkv1
                            kvbf_cnt[h] = nkv1
                            p.enabled = en2

                gate()
                c1 = p.snap()
                nred = p.emit("vector", lambda e: e.reduce_sum(stat[:, 0:NJ], ssc[:], axis=AX.X),
                              waits=[("act", c1["act"])], inc="dve", tag="rms red")
                nms = p.emit("vector", lambda e: e.tensor_scalar(stat[:, 8:8 + NJ], stat[:, 0:NJ],
                                                                 1.0 / D, 1e-6, ALU.mult, ALU.add),
                             waits=[("dve", nred)], inc="dve", tag="rms ms")
                nrc = p.emit("vector", lambda e: e.reciprocal(stat[:, 16:16 + NJ], stat[:, 8:8 + NJ]),
                             waits=[("dve", nms)], inc="dve", tag="rms recip")
                nsq = p.emit("scalar", lambda e: e.activation(stat[:, 24:24 + NJ], stat[:, 16:16 + NJ], AF.Sqrt),
                             waits=[("dve", nrc)], inc="act", tag="rms sqrt")
                o2s = []
                for blk in range(NJ):
                    o2s.append(p.emit("vector",
                                      lambda e, blk=blk: e.tensor_scalar(
                                          o2_tm[:, blk, :], o2_tm[:, blk, :],
                                          stat[:, 24 + blk:25 + blk], None, ALU.mult),
                                      waits=[("act", nsq), ("dve", c1["dve"])], inc="dve", tag=f"o2s {blk}"))
                dbg_dump("o2", poolC[:], "dve", it)

                transpose_set(lambda c, jj: o2_tm[:, jj, c * 128:(c + 1) * 128],
                              lambda c, jj: o2fm[:, c, jj * 128:(jj + 1) * 128],
                              H, NJ, "o2", waits_extra=[("dve", max(o2s))])
                o2fm_done = p.cnt["dve"]

                def wout_out(jj, s, bank, nmm):
                    nv = p.emit("vector",
                                lambda e, jj=jj, s=s, bank=bank: e.tensor_copy(
                                    s_tm[:, jj, s * 256:(s + 1) * 256], bank[:, 0:256]),
                                waits=[("pe", nmm)], inc="dve", tag=f"s ev {jj}.{s}")
                    bfree(bank, dve=nv)
                proj_tm("wo0", range(8), o2fm, 0, wout_out, pass0, "wo0", [("dve", o2fm_done)])
                s0_done = p.cnt["dve"]
                dbg_dump("s0", poolD[:], "dve", it)

                gate()
                prev_hdn = 0
                for cs in range(44):
                    view, wws, slotid, extra = wm.load("wgu0", cs, pass0)
                    fo = cs
                    bg, bu = pb[2 * (cs % 2)], pb[2 * (cs % 2) + 1]
                    def gmm(e, view=view, bg=bg):
                        r = None
                        for fi in range(H):
                            r = e.matmul(bg[:], view[:, fi, 0:128], xn[:, fi, 3:515],
                                         start=(fi == 0), stop=(fi == H - 1))
                        return r
                    ng = p.emit("tensor", gmm,
                                waits=wws + [("dve", xn_done)] + list(bank_free[id(bg)].items()),
                                inc="pe", tag=f"f0g {fo}")
                    def umm(e, view=view, bu=bu):
                        r = None
                        for fi in range(H):
                            r = e.matmul(bu[:], view[:, fi, 128:256],
                                         xn[:, fi, 3:515], start=(fi == 0), stop=(fi == H - 1))
                        return r
                    nu = p.emit("tensor", umm,
                                waits=wws + list(bank_free[id(bu)].items()),
                                inc="pe", tag=f"f0u {fo}")
                    na = p.emit("scalar",
                                lambda e, fo=fo, bg=bg: e.activation(sil_scr[:], bg[:], AF.Silu,
                                                                     scale=gsc[:, fo:fo + 1]),
                                waits=[("pe", ng), ("dve", prev_hdn)], inc="act", tag=f"f0sil {fo}")
                    prev_hdn = p.emit("vector",
                                      lambda e, fo=fo, bu=bu: e.scalar_tensor_tensor(
                                          hdn[:, fo, :], bu[:], usc[:, fo:fo + 1], sil_scr[:],
                                          ALU.mult, ALU.mult),
                                      waits=[("act", na), ("pe", nu)], inc="dve", tag=f"f0hdn {fo}")
                    bfree(bg, act=na)
                    bfree(bu, dve=prev_hdn)
                    wm.release(slotid, [("pe", nu)], extra)
                hdn_all = p.cnt["dve"]

                gate()
                g0_last = 0
                for s in range(16):
                    bank = pb[s % 2]
                    nmm = 0
                    for half in range(2):
                        view, wws, slotid, extra = wm.load("wd0", 2 * s + half, pass0)
                        def dmm(e, view=view, bank=bank, half=half):
                            r = None
                            for c in range(22):
                                r = e.matmul(bank[:], view[:, c, :], hdn[:, 22 * half + c, :],
                                             start=(half == 0 and c == 0), stop=(half == 1 and c == 21))
                            return r
                        nmm = p.emit("tensor", dmm,
                                     waits=wws + [("dve", hdn_all)] + list(bank_free[id(bank)].items()),
                                     inc="pe", tag=f"d0 {s}.{half}")
                        wm.release(slotid, [("pe", nmm)], extra)
                    g0_last = p.emit("scalar",
                                     lambda e, s=s, bank=bank: e.activation(g_fm[:, s, :], bank[:], AF.Silu,
                                                                            scale=dsc[:, s:s + 1]),
                                     waits=[("pe", nmm)], inc="act", tag=f"d0sil {s}")
                    bfree(bank, act=g0_last)
                dbg_dump("g0", poolE[:], "act", it)

                gate()
                resid_stage(it, "x", "x1", g0_last, s0_done)

            if 1 in layers:
                gate()
                cpre = p.snap()
                p.emit("vector", lambda e: e.tensor_copy(xn[:, :, 0:3], halo[:]),
                       waits=[("pe", cpre["pe"]), ("pool", p.cnt["pool"])],
                       inc="dve", tag="halo in")
                ln_stage(it, "x1")
                p.emit("vector", lambda e: e.tensor_copy(halo[:], xn[:, :, 512:515]),
                       waits=[("dve", p.cnt["dve"])], inc="dve", tag="halo out")
                xn_done = p.cnt["dve"]
                dbg_dump("xn1", xn[:].rearrange("p a b -> p (a b)"), "dve", it)

                gate()
                prev_sil = 0
                for c in range(H):
                    n1 = p.emit("vector",
                                lambda e, c=c: e.tensor_scalar(scrA[:, 0:512], xn[:, c, 0:512],
                                                               cw[:, c, 0:1], None, ALU.mult),
                                waits=[("dve", xn_done), ("act", prev_sil)], inc="dve", tag=f"cv0 {c}")
                    for k in range(1, 4):
                        n1 = p.emit("vector",
                                    lambda e, c=c, k=k: e.scalar_tensor_tensor(
                                        scrA[:, 0:512], xn[:, c, k:k + 512], cw[:, c, k:k + 1],
                                        scrA[:, 0:512], ALU.mult, ALU.add),
                                    waits=[("dve", n1)], inc="dve", tag=f"cv{k} {c}")
                    prev_sil = p.emit("scalar",
                                      lambda e, c=c: e.activation(xconv[:, c, :], scrA[:, 0:512], AF.Silu,
                                                                  bias=cb[:, c:c + 1]),
                                      waits=[("dve", n1)], inc="act", tag=f"cvsil {c}")
                xconv_done = prev_sil
                dbg_dump("xconv", poolC[:], "act", it)

                gate()
                k_n, v_b, kv_tm, q_n = slot

                def norm_proj(w_name, dst, tag):
                    red_cnt = [0]
                    def cb_(jj, s, bank, nmm, dst=dst):
                        nv = p.emit("vector",
                                    lambda e, jj=jj, s=s, bank=bank: e.tensor_copy(
                                        dst[:, jj, s * 256:(s + 1) * 256], bank[:, 0:256]),
                                    waits=[("pe", nmm)], inc="dve", tag=f"{tag} ev {jj}.{s}")
                        na = p.emit("scalar",
                                    lambda e, bank=bank: e.activation(scrB[:, 0:256], bank[:, 0:256], AF.Square),
                                    waits=[("pe", nmm), ("dve", red_cnt[-1]), ("dve", nv)],
                                    inc="act", tag=f"{tag} sq")
                        nr = p.emit("vector",
                                    lambda e, jj=jj, s=s: e.reduce_sum(
                                        nrm[:, jj * H + s * 2:jj * H + s * 2 + 2],
                                        scrB[:, 0:256].rearrange("p (a b) -> p a b", a=2), axis=AX.X),
                                    waits=[("act", na)], inc="dve", tag=f"{tag} red")
                        red_cnt.append(nr)
                        bfree(bank, dve=nv, act=na)
                    proj_tm(w_name, range(8), xn, 3, cb_, pass0, tag, [("dve", xn_done)])
                    nq1 = p.emit("scalar", lambda e: e.activation(stat[:, 0:NJ * H], nrm[:], AF.Sqrt),
                                 waits=[("dve", red_cnt[-1])], inc="act", tag=f"{tag} nsq")
                    nq2 = p.emit("vector", lambda e: e.tensor_scalar(stat[:, 0:NJ * H], stat[:, 0:NJ * H],
                                                                     1e-12, None, ALU.max),
                                 waits=[("act", nq1)], inc="dve", tag=f"{tag} nmax")
                    nq3 = p.emit("vector", lambda e: e.reciprocal(nrm[:], stat[:, 0:NJ * H]),
                                 waits=[("dve", nq2)], inc="dve", tag=f"{tag} nrec")
                    last = 0
                    for jj in range(NJ):
                        for h in range(H):
                            last = p.emit("vector",
                                          lambda e, jj=jj, h=h, dst=dst: e.tensor_scalar(
                                              dst[:, jj, h * 128:(h + 1) * 128],
                                              dst[:, jj, h * 128:(h + 1) * 128],
                                              nrm[:, jj * H + h:jj * H + h + 1], None, ALU.mult),
                                          waits=[("dve", nq3)], inc="dve", tag=f"{tag} ap {jj}.{h}")
                    return last

                k_done = norm_proj("wk1", k_n, "k1")
                def v_out1(jj, s, bank, nmm):
                    nv = p.emit("vector",
                                lambda e, jj=jj, s=s, bank=bank: e.tensor_copy(
                                    v_b[:, jj, s * 256:(s + 1) * 256], bank[:, 0:256]),
                                waits=[("pe", nmm)], inc="dve", tag=f"v1 ev {jj}.{s}")
                    bfree(bank, dve=nv)
                proj_tm("wv1", range(8), xn, 3, v_out1, pass0, "v1", [("dve", xn_done)])
                v_done = p.cnt["dve"]
                kv_done = []
                for jj in range(NJ):
                    kv_done.append(p.emit("vector",
                                          lambda e, jj=jj: e.tensor_tensor(
                                              kv_tm[:, jj, :], k_n[:, jj, :], v_b[:, jj, :], ALU.mult),
                                          waits=[("dve", max(k_done, v_done))], inc="dve", tag=f"kv1 {jj}"))
                dbg_dump("kvtm", poolA[:, 16384:24576], "dve", it)
                q_done = norm_proj("wq1", q_n, "q1")

                def gate_out(jj, s, bank, nmm):
                    nv = p.emit("vector",
                                lambda e, jj=jj, s=s, bank=bank: e.scalar_tensor_tensor(
                                    scrC[:, 0:256], gb_tbl[:, s * 256:(s + 1) * 256], 1.0, bank[:, 0:256],
                                    ALU.bypass, ALU.add),
                                waits=[("pe", nmm), ("act", p.cnt["act"])], inc="dve", tag=f"ga {jj}.{s}")
                    na = p.emit("scalar",
                                lambda e, jj=jj, s=s: e.activation(
                                    gate_tm[:, jj, s * 256:(s + 1) * 256], scrC[:, 0:256], AF.Sigmoid),
                                waits=[("dve", nv)], inc="act", tag=f"gs {jj}.{s}")
                    bfree(bank, dve=nv)
                proj_tm("wgt1", range(8), xconv, 0, gate_out, pass0, "gt1", [("act", xconv_done)])
                gate_done = p.cnt["act"]

                gate()
                scol = []
                for c in range(H):
                    evs = []
                    for jj in range(NJ):
                        sl = (c * NJ + jj) % 2
                        ntr = p.emit("tensor",
                                     lambda e, c=c, jj=jj, sl=sl: e.transpose(
                                         pt_slot[sl], kv_tm[:, jj, c * 128:(c + 1) * 128], id_bf[:]),
                                     waits=[("dve", max(kv_done)), ("dve", tr_ev[0]), ("dve", tr_ev[1])],
                                     inc="pe", tag=f"kvtr {c}.{jj}")
                        ne = p.emit("vector",
                                    lambda e, jj=jj, sl=sl: e.tensor_copy(
                                        scrB[:, jj * 128:(jj + 1) * 128], pt_slot[sl]),
                                    waits=[("pe", ntr), ("act", gate_done)], inc="dve", tag=f"kvev {c}.{jj}")
                        tr_ev[sl] = ne
                        evs.append(ne)
                    g_ap = bass.AP(gbc, c, [[H, 128], [0, 512]])
                    nsc = p.emit("vector",
                                 lambda e, c=c, g_ap=g_ap: e.tensor_tensor_scan(
                                     scrC[:], g_ap, scrB[:], s_state[:, c:c + 1], ALU.mult, ALU.add),
                                 waits=[("dve", max(evs))], inc="dve", tag=f"scan {c}")
                    p.emit("vector", lambda e, c=c: e.tensor_copy(s_state[:, c:c + 1], scrC[:, 511:512]),
                           waits=[("dve", nsc)], inc="dve", tag=f"scan st {c}")
                    scol.append(p.emit("vector", lambda e, c=c: e.tensor_copy(s_fm[:, c, :], scrC[:]),
                                       waits=[("dve", nsc)], inc="dve", tag=f"scan cp {c}"))
                dbg_dump("sfm", poolC[:], "dve", it)

                transpose_set(lambda c, jj: s_fm[:, c, jj * 128:(jj + 1) * 128],
                              lambda c, jj: v_b[:, jj, c * 128:(c + 1) * 128],
                              H, NJ, "sT", waits_extra=[("dve", max(scol))])
                stm_done = p.cnt["dve"]
                omx = []
                for jj in range(NJ):
                    n1 = p.emit("vector",
                                lambda e, jj=jj: e.tensor_tensor(
                                    k_n[:, jj, :], v_b[:, jj, :], gate_tm[:, jj, :], ALU.mult),
                                waits=[("dve", stm_done)], inc="dve", tag=f"om1 {jj}")
                    omx.append(p.emit("vector",
                                      lambda e, jj=jj: e.tensor_tensor(
                                          kv_tm[:, jj, :], k_n[:, jj, :], q_n[:, jj, :], ALU.mult),
                                      waits=[("dve", n1), ("dve", q_done)], inc="dve", tag=f"om2 {jj}"))
                dbg_dump("omix", poolA[:, 16384:24576], "dve", it)
                transpose_set(lambda c, jj: kv_tm[:, jj, c * 128:(c + 1) * 128],
                              lambda c, jj: omix_fm[:, c, jj * 128:(jj + 1) * 128],
                              H, NJ, "om", waits_extra=[("dve", max(omx))])
                omixfm_done = p.cnt["dve"]

                gate()
                ybf = {}
                def y_out(jj, s, bank, nmm):
                    nv = p.emit("vector",
                                lambda e, jj=jj, s=s, bank=bank: e.tensor_copy(
                                    y_bf[:, jj, s * 256:(s + 1) * 256], bank[:, 0:256]),
                                waits=[("pe", nmm)], inc="dve", tag=f"y ev {jj}.{s}")
                    ybf[(jj, s)] = nv
                    bfree(bank, dve=nv)
                proj_tm("wo1", range(8), omix_fm, 0, y_out, pass0, "wo1", [("dve", omixfm_done)])
                prev_m3 = 0
                for jj in range(NJ):
                    w0 = max(ybf[(jj, s)] for s in range(8))
                    na = p.emit("scalar",
                                lambda e, jj=jj: e.activation(junkD(jj * 2048, 2048), y_bf[:, jj, :],
                                                              AF.Square, accum_out=stat[:, jj:jj + 1]),
                                waits=[("dve", w0)], inc="act", tag=f"y sq {jj}")
                    ns = p.emit("vector", lambda e, jj=jj: e.reduce_sum(stat[:, 8 + jj:9 + jj],
                                                                        y_bf[:, jj, :], axis=AX.X),
                                waits=[("dve", w0)], inc="dve", tag=f"y sum {jj}")
                    nv1 = p.emit("vector", lambda e, jj=jj: e.tensor_scalar(
                        stat[:, 16 + jj:17 + jj], stat[:, 8 + jj:9 + jj], 1.0 / D, None, ALU.mult),
                                 waits=[("dve", ns)], inc="dve", tag="y mean")
                    nv2 = p.emit("vector", lambda e, jj=jj: e.tensor_tensor(
                        stat[:, 24 + jj:25 + jj], stat[:, 16 + jj:17 + jj],
                        stat[:, 16 + jj:17 + jj], ALU.mult),
                                 waits=[("dve", nv1)], inc="dve", tag="y m2")
                    nv3 = p.emit("vector", lambda e, jj=jj: e.scalar_tensor_tensor(
                        stat[:, 32 + jj:33 + jj], stat[:, jj:jj + 1], 1.0 / D,
                        stat[:, 24 + jj:25 + jj], ALU.mult, ALU.subtract),
                                 waits=[("act", na), ("dve", nv2)], inc="dve", tag="y var")
                    nv4 = p.emit("vector", lambda e, jj=jj: e.tensor_scalar(
                        stat[:, 40 + jj:41 + jj], stat[:, 32 + jj:33 + jj], 1e-5, None, ALU.add),
                                 waits=[("dve", nv3)], inc="dve", tag="y eps")
                    nv5 = p.emit("vector", lambda e, jj=jj: e.reciprocal(
                        stat[:, 48 + jj:49 + jj], stat[:, 40 + jj:41 + jj]),
                                 waits=[("dve", nv4)], inc="dve", tag="y recip")
                    na2 = p.emit("scalar", lambda e, jj=jj: e.activation(
                        stat[:, 56 + jj:57 + jj], stat[:, 48 + jj:49 + jj], AF.Sqrt),
                                 waits=[("dve", nv5)], inc="act", tag="y sqrt")
                    for q4 in range(4):
                        o = q4 * 512
                        m1 = p.emit("vector",
                                    lambda e, jj=jj, o=o: e.tensor_scalar(
                                        scrA[:], y_bf[:, jj, o:o + 512],
                                        stat[:, 16 + jj:17 + jj], stat[:, 56 + jj:57 + jj],
                                        ALU.subtract, ALU.mult),
                                    waits=[("act", na2), ("dve", prev_m3)], inc="dve", tag=f"y ap1 {jj}.{q4}")
                        m2 = p.emit("vector",
                                    lambda e, o=o: e.scalar_tensor_tensor(
                                        scrA[:], lnw_tbl[:, o:o + 512], 1.0, scrA[:], ALU.bypass, ALU.mult),
                                    waits=[("dve", m1)], inc="dve", tag=f"y ap2 {jj}.{q4}")
                        prev_m3 = p.emit("vector",
                               lambda e, jj=jj, o=o: e.scalar_tensor_tensor(
                                   s_tm[:, jj, o:o + 512], lnb_tbl[:, o:o + 512], 1.0, scrA[:],
                                   ALU.bypass, ALU.add),
                               waits=[("dve", m2)], inc="dve", tag=f"y ap3 {jj}.{q4}")
                s1_done = p.cnt["dve"]
                dbg_dump("s1", poolD[:], "dve", it)

                gate()
                prev_nh = 0
                for cs in range(64):
                    view, wws, slotid, extra = wm.load("wgu1", cs, pass0)
                    c1x = cs
                    bg, bu = pb[2 * (cs % 2)], pb[2 * (cs % 2) + 1]
                    def gmm(e, view=view, bg=bg):
                        r = None
                        for fi in range(H):
                            r = e.matmul(bg[:], view[:, fi, 0:128], xn[:, fi, 3:515],
                                         start=(fi == 0), stop=(fi == H - 1))
                        return r
                    ng = p.emit("tensor", gmm,
                                waits=wws + [("dve", xn_done)] + list(bank_free[id(bg)].items()),
                                inc="pe", tag=f"f1g {c1x}")
                    def umm(e, view=view, bu=bu):
                        r = None
                        for fi in range(H):
                            r = e.matmul(bu[:], view[:, fi, 128:256],
                                         xn[:, fi, 3:515], start=(fi == 0), stop=(fi == H - 1))
                        return r
                    nu = p.emit("tensor", umm,
                                waits=wws + list(bank_free[id(bu)].items()),
                                inc="pe", tag=f"f1u {c1x}")
                    na = p.emit("scalar", lambda e, bg=bg: e.activation(sil_scr[:], bg[:], AF.Silu),
                                waits=[("pe", ng), ("dve", prev_nh)], inc="act", tag=f"f1sil {c1x}")
                    prev_nh = p.emit("vector",
                                     lambda e, c1x=c1x, bu=bu: e.scalar_tensor_tensor(
                                         hdn[:, c1x, :], sil_scr[:], 1.0, bu[:], ALU.bypass, ALU.mult),
                                     waits=[("act", na), ("pe", nu)], inc="dve", tag=f"f1hdn {c1x}")
                    bfree(bg, act=na)
                    bfree(bu, dve=prev_nh)
                    wm.release(slotid, [("pe", nu)], extra)
                hdn1_all = p.cnt["dve"]

                gate()
                g1_last = 0
                for s in range(16):
                    bank = pb[s % 2]
                    nmm = 0
                    for half in range(2):
                        view, wws, slotid, extra = wm.load("wd1", 2 * s + half, pass0)
                        def dmm(e, view=view, bank=bank, half=half):
                            r = None
                            for c in range(32):
                                r = e.matmul(bank[:], view[:, c, :], hdn[:, 32 * half + c, :],
                                             start=(half == 0 and c == 0), stop=(half == 1 and c == 31))
                            return r
                        nmm = p.emit("tensor", dmm,
                                     waits=wws + [("dve", hdn1_all)] + list(bank_free[id(bank)].items()),
                                     inc="pe", tag=f"d1 {s}.{half}")
                        wm.release(slotid, [("pe", nmm)], extra)
                    g1_last = p.emit("scalar",
                                     lambda e, s=s, bank=bank: e.activation(g_fm[:, s, :], bank[:], AF.Silu),
                                     waits=[("pe", nmm)], inc="act", tag=f"d1sil {s}")
                    bfree(bank, act=g1_last)
                dbg_dump("g1", poolE[:], "act", it)

                gate()
                resid_stage(it, "x1", "out" if mode != "l0" else "x1", g1_last, s1_done)

        try:
            _tile_loop_body()
        except _StopEmit:
            pass

        fin = p.snap()

        def final_waits(e):
            e.wait_ge(sems["da"], fin["da"])
            e.wait_ge(sems["ds"], fin["ds"])
            return e.wait_ge(sems["dg"], fin["dg"])
        p.emit("sync", final_waits, tag="final")

        p.check()

        with nc.Block() as block:
            def runner(ops):
                def f(eng):
                    for op in ops:
                        op[0](eng)
                return f
            block.tensor(runner(p.q["tensor"]))
            block.vector(runner(p.q["vector"]))
            block.scalar(runner(p.q["scalar"]))
            block.gpsimd(runner(p.q["gpsimd"]))
            block.sync(runner(p.q["sync"]))

    return nc


def kernel(**inputs):
    W, C = _host_prepare(inputs)
    x = np.ascontiguousarray(np.asarray(inputs["L0_attn_x"], np.float32))

    nc = build_program(MODE, DEBUG)
    in_maps = []
    for core in range(NCORE):
        m = {"x": x[core]}
        m.update(W)
        m.update(C)
        in_maps.append(m)
    res = run_bass_kernel_spmd(nc, in_maps, core_ids=list(range(NCORE)))
    return np.stack([np.asarray(res.results[i]["out"]) for i in range(NCORE)], axis=0)


# version 03 (final)
# speedup vs baseline: 1.4793x; 1.0831x over previous
import contextlib
import numpy as np
import concourse.bass as bass
import concourse.mybir as mybir
from concourse.bass_utils import run_bass_kernel_spmd

F32 = mybir.dt.float32
BF16 = mybir.dt.bfloat16
AF = mybir.ActivationFunctionType
ALU = mybir.AluOpType
AX = mybir.AxisListType

D = 2048
T = 4096
MT = 512
NTILE = T // MT
NJ = MT // 128
H = 16
NG = 4
I0C = 44
I1C = 64
NCORE = 8

MODE = "full"
DEBUG = False
KTILES = NTILE
KSTAGES = 10**9
KLNPARTS = 99
STRICT_TR = True
KT2PARTS = 99

ENGINES = ("tensor", "vector", "scalar", "gpsimd", "sync")
DMA_SEMS = {"gpsimd": "dg", "sync": "ds", "scalar": "da"}


class Prog:
    def __init__(self, nc):
        self.nc = nc
        self.q = {e: [] for e in ENGINES}
        self.cnt = {s: 0 for s in ("pe", "dve", "act", "pool", "dg", "ds", "da")}
        self.sem = {}
        self.pending = {e: [] for e in ENGINES}
        self.enabled = True

    def emit(self, eng, fn, waits=(), inc=None, inc_n=1, tag=""):
        if not self.enabled:
            return self.cnt[inc] if inc is not None else None
        w = {}
        for s, c in list(self.pending[eng]) + list(waits):
            if c > 0:
                w[s] = max(w.get(s, 0), c)
        self.pending[eng] = []
        w = tuple(w.items())
        sem = self.sem
        def op(e, _fn=fn, _w=w, _inc=inc, _n=inc_n):
            for s, c in _w:
                e.wait_ge(sem[s], c)
            r = _fn(e)
            if _inc is not None and r is not None:
                r.then_inc(sem[_inc], _n)
        self.q[eng].append((op, w, inc, inc_n, tag))
        if inc is not None:
            self.cnt[inc] += inc_n
            return self.cnt[inc]
        return None

    def dma(self, eng, out_ap, in_ap, waits=(), tag=""):
        s = DMA_SEMS[eng]
        return self.emit(eng, lambda e, o=out_ap, i=in_ap: e.dma_start(out=o, in_=i),
                         waits=list(waits) + [(s, self.cnt[s])], inc=s, inc_n=16, tag=tag)

    def snap(self):
        return dict(self.cnt)

    def barrier(self, snap=None):
        c = snap or self.snap()
        for e in ENGINES:
            self.pending[e] = [(s, v) for s, v in c.items() if v > 0]
        return c

    def check(self):
        idx = {e: 0 for e in ENGINES}
        cur = {s: 0 for s in self.cnt}
        progress = True
        while progress:
            progress = False
            for e in ENGINES:
                while idx[e] < len(self.q[e]):
                    op, w, inc, inc_n, tag = self.q[e][idx[e]]
                    if any(cur[s] < c for s, c in w):
                        break
                    if inc is not None:
                        cur[inc] += inc_n
                    idx[e] += 1
                    progress = True
        stuck = {e: (idx[e], len(self.q[e])) for e in ENGINES if idx[e] < len(self.q[e])}
        if stuck:
            det = []
            for e, (i, n) in stuck.items():
                op, w, inc, inc_n, tag = self.q[e][i]
                miss = [(s, c, cur[s]) for s, c in w if cur[s] < c]
                det.append(f"{e}[{i}/{n}] tag={tag} missing={miss}")
            raise RuntimeError("build-time deadlock: " + "; ".join(det))
        for e in ENGINES:
            for op, w, inc, inc_n, tag in self.q[e]:
                for s, c in w:
                    if c > self.cnt[s]:
                        raise RuntimeError(f"wait beyond final: {e} tag={tag} {s}>={c} final={self.cnt[s]}")


def _tile_w(w, fo_slab):
    FO, FI = w.shape
    ns, c = FO // fo_slab, FI // 128
    a = np.asarray(w, dtype=np.float32).reshape(ns, fo_slab, c, 128)
    return np.ascontiguousarray(a.transpose(0, 3, 2, 1))


def _host_prepare(inp):
    W = {}
    def _half_fi(a):
        ns, p, c, f = a.shape
        return np.ascontiguousarray(a.reshape(ns, p, 2, c // 2, f).transpose(0, 2, 1, 3, 4)
                                    ).reshape(2 * ns, p, c // 2, f)
    W["wqkvu"] = _tile_w(np.asarray(inp["L0_attn_W_qkvu"]), 256)
    W["wo0"] = _tile_w(np.asarray(inp["L0_attn_W_out"]), 256)
    g0 = _tile_w(np.asarray(inp["L0_ffn_gate_w"]), 128)
    u0 = _tile_w(np.asarray(inp["L0_ffn_up_w"]), 128)
    W["wgu0"] = np.ascontiguousarray(np.concatenate([g0, u0], axis=3))
    W["wd0"] = _half_fi(_tile_w(np.asarray(inp["L0_ffn_down_w"]), 128))
    W["wq1"] = _tile_w(np.asarray(inp["L1_attn_Wq"]), 256)
    W["wk1"] = _tile_w(np.asarray(inp["L1_attn_Wk"]), 256)
    W["wv1"] = _tile_w(np.asarray(inp["L1_attn_Wv"]), 256)
    W["wgt1"] = _tile_w(np.asarray(inp["L1_attn_gate_w"]), 256)
    W["wo1"] = _tile_w(np.asarray(inp["L1_attn_Wo"]), 256)
    g1 = _tile_w(np.asarray(inp["L1_ffn_w_gate"]), 128)
    u1 = _tile_w(np.asarray(inp["L1_ffn_w_up"]), 128)
    W["wgu1"] = np.ascontiguousarray(np.concatenate([g1, u1], axis=3))
    W["wd1"] = _half_fi(_tile_w(np.asarray(inp["L1_ffn_w_down"]), 128))

    C = {}
    i = np.arange(128, dtype=np.float32)
    dd = i[None, :] - i[:, None]
    C["c_ddiag"] = np.where(dd >= 0, dd, 3.0e4).astype(np.float32)
    C["c_dqfull"] = np.broadcast_to(i[None, :] + 1.0, (128, 128)).astype(np.float32).copy()
    C["c_drev"] = (127.0 - i)[:, None].astype(np.float32).copy()
    C["c_eye"] = np.eye(128, dtype=np.float32)
    C["c_slope"] = np.asarray(inp["L0_attn_slope_rate"], np.float32).reshape(1, H).copy()
    C["c_gamma"] = np.asarray(inp["L1_attn_gamma"], np.float32).reshape(1, H).copy()
    C["c_gbrow"] = np.asarray(inp["L1_attn_gate_b"], np.float32).reshape(1, D).copy()
    C["c_lnwrow"] = np.asarray(inp["L1_attn_ln_w"], np.float32).reshape(1, D).copy()
    C["c_lnbrow"] = np.asarray(inp["L1_attn_ln_b"], np.float32).reshape(1, D).copy()
    cw = np.asarray(inp["L1_attn_conv_w"], np.float32).reshape(H, 128, 4)
    C["c_cw"] = np.ascontiguousarray(cw.transpose(1, 0, 2)).reshape(128, H * 4)
    cb = np.asarray(inp["L1_attn_conv_b"], np.float32).reshape(H, 128)
    C["c_cb"] = np.ascontiguousarray(cb.T)
    C["c_gs"] = np.ascontiguousarray(np.asarray(inp["L0_ffn_gate_s"], np.float32).reshape(I0C, 128).T)
    C["c_us"] = np.ascontiguousarray(np.asarray(inp["L0_ffn_up_s"], np.float32).reshape(I0C, 128).T)
    C["c_ds"] = np.ascontiguousarray(np.asarray(inp["L0_ffn_down_s"], np.float32).reshape(16, 128).T)
    return W, C


SLAB_SHAPES = {
    "wqkvu": (32, 16, 256), "wo0": (8, 16, 256), "wgu0": (44, 16, 256),
    "wd0": (32, 22, 128), "wq1": (8, 16, 256),
    "wk1": (8, 16, 256), "wv1": (8, 16, 256), "wgt1": (8, 16, 256),
    "wo1": (8, 16, 256), "wgu1": (64, 16, 256),
    "wd1": (32, 32, 128),
}
L0_W = ("wqkvu", "wo0", "wgu0", "wd0")


class WeightMgr:
    def __init__(self, p, wf32, wbf16, wsl):
        self.p = p
        self.wf32 = wf32
        self.wbf16 = wbf16
        self.wsl = wsl
        self.slot_free = [dict() for _ in wsl]
        self.turn = 0
        self.written_back = set()

    def load(self, name, s, pass0):
        p = self.p
        slot = self.turn
        self.turn = (self.turn + 1) % len(self.wsl)
        nsl, c, cols = SLAB_SHAPES[name]
        view = self.wsl[slot][:, 0:c * cols].rearrange("p (c f) -> p c f", c=c, f=cols)
        waits = list(self.slot_free[slot].items())
        key = (name, s)
        if pass0 and key not in self.written_back:
            n = p.dma("gpsimd", view, self.wf32[name][s], waits=waits, tag=f"ld0 {name}{s}")
            wb = p.dma("sync", self.wbf16[name][s], view, waits=[("dg", n)], tag=f"wb {name}{s}")
            self.written_back.add(key)
            return view, [("dg", n)], slot, [("ds", wb)]
        n = p.dma("sync", view, self.wbf16[name][s], waits=waits, tag=f"ld {name}{s}")
        return view, [("ds", n)], slot, []

    def release(self, slot, waits, extra=()):
        d = {}
        for s, c in list(waits) + list(extra):
            d[s] = max(d.get(s, 0), c)
        self.slot_free[slot] = d


def build_program(mode, debug=False):
    nc = bass.Bass("TRN2", target_bir_lowering=False)
    layers = {"l0": (0,), "l1": (1,), "full": (0, 1)}[mode]

    x_in = nc.dram_tensor("x", [T, D], F32, kind="ExternalInput").ap() if 0 in layers else None
    if mode == "full":
        x1 = nc.dram_tensor("x1buf", [T, D], F32, kind="Internal").ap()
        out = nc.dram_tensor("out", [T, D], F32, kind="ExternalOutput").ap()
    elif mode == "l0":
        x1 = nc.dram_tensor("out", [T, D], F32, kind="ExternalOutput").ap()
        out = None
    else:
        x1 = nc.dram_tensor("x1", [T, D], F32, kind="ExternalInput").ap()
        out = nc.dram_tensor("out", [T, D], F32, kind="ExternalOutput").ap()

    wf32, wbf16 = {}, {}
    for name, (nsl, c, cols) in SLAB_SHAPES.items():
        if ((0 in layers) if name in L0_W else (1 in layers)):
            wf32[name] = nc.dram_tensor(name, [nsl, 128, c, cols], F32, kind="ExternalInput").ap()
            wbf16[name] = nc.dram_tensor(name + "_bf", [nsl, 128, c, cols], BF16, kind="Internal").ap()

    cN = {"c_ddiag": [128, 128], "c_dqfull": [128, 128], "c_drev": [128, 1],
          "c_eye": [128, 128], "c_slope": [1, H], "c_gamma": [1, H],
          "c_gbrow": [1, D], "c_lnwrow": [1, D], "c_lnbrow": [1, D],
          "c_cw": [128, H * 4], "c_cb": [128, H], "c_gs": [128, I0C],
          "c_us": [128, I0C], "c_ds": [128, 16]}
    cin = {k: nc.dram_tensor(k, v, F32, kind="ExternalInput").ap() for k, v in cN.items()}

    dbg = {}
    def dbg_def(name, shape, dt=BF16):
        if debug:
            dbg[name] = nc.dram_tensor("dbg_" + name, [NTILE] + shape, dt, kind="ExternalOutput").ap()
    dbg_def("xn0", [128, H * 515]); dbg_def("s0", [128, NJ * D]); dbg_def("g0", [128, 16 * 512])
    dbg_def("o2", [128, NJ * D]); dbg_def("xn1", [128, H * 515]); dbg_def("xconv", [128, H * 512])
    dbg_def("kvtm", [128, NJ * D]); dbg_def("sfm", [128, H * 512]); dbg_def("omix", [128, NJ * D])
    dbg_def("s1", [128, NJ * D]); dbg_def("g1", [128, 16 * 512])

    stack = contextlib.ExitStack()
    sbuf = lambda name, shape, dt: stack.enter_context(nc.sbuf_tensor(name, shape, dt))
    psum = lambda name, shape, dt: stack.enter_context(nc.psum_tensor(name, shape, dt))

    with stack:
        xst0 = sbuf("xst0", [128, D], F32)
        xn = sbuf("xn", [128, H, 515], BF16)
        halo = sbuf("halo", [128, H, 3], BF16)
        poolA = sbuf("poolA", [128, 32768], BF16)
        poolB = sbuf("poolB", [128, 10240], BF16)
        poolC = sbuf("poolC", [128, 8192], BF16)
        poolD = sbuf("poolD", [128, 8192], BF16)
        poolE = sbuf("poolE", [128, 8192], BF16)
        kv_bf = sbuf("kv_bf", [128, D], BF16)
        s_state = sbuf("s_state", [128, H], F32)
        wsl = [sbuf("wsl0", [128, 4096], BF16), sbuf("wsl1", [128, 4096], BF16)]
        id_bf = sbuf("id_bf", [128, 128], BF16)
        masks = sbuf("masks", [128, H, 128], BF16)
        qdt = sbuf("qdt", [128, H * 128], BF16)
        rbc = sbuf("rbc", [128, H], F32)
        gbc = sbuf("gbc", [128, H], F32)
        bdc = sbuf("bdc", [128, H], F32)
        kdc = sbuf("kdc", [128, H], F32)
        cw = sbuf("cw", [128, H, 4], F32)
        cb = sbuf("cb", [128, H], F32)
        gsc = sbuf("gsc", [128, I0C], F32)
        usc = sbuf("usc", [128, I0C], F32)
        dsc = sbuf("dsc", [128, 16], F32)
        gb_tbl = sbuf("gb_tbl", [128, D], BF16)
        lnw_tbl = sbuf("lnw_tbl", [128, D], BF16)
        lnb_tbl = sbuf("lnb_tbl", [128, D], BF16)
        scrA = sbuf("scrA", [128, 512], F32)
        scrB = sbuf("scrB", [128, 512], F32)
        scrC = sbuf("scrC", [128, 512], F32)
        sil_scr = sbuf("sil_scr", [128, 512], BF16)
        hstage = sbuf("hstage", [128, 512], BF16)
        stat = sbuf("stat", [128, 64], F32)
        nrm = sbuf("nrm", [128, NJ * H], F32)
        ssc = sbuf("ssc", [128, NJ, H], F32)
        onesf = scrB[0:1, 128:256]
        rowbuf = scrC[0:1, 0:512]
        dd_sb = scrA[:, 0:128]
        dq_sb = scrA[:, 128:256]
        drev_sb = scrA[:, 256:257]
        tmp128 = scrB[:, 0:128]
        ktk = [sil_scr[:, 256:384], sil_scr[:, 384:512]]
        qkm = [sil_scr[:, 0:128], sil_scr[:, 128:256]]
        sgt = scrC[:, 0:128]

        pb = [psum(f"pb{i}", [128, 512], F32) for i in range(4)]
        pqk_b = psum("pqk", [128, 512], F32)
        po_b = psum("po", [128, 512], F32)
        pkv_b = psum("pkv", [128, 512], F32)
        ptr = psum("ptr", [128, 1024], BF16)
        pqk = pqk_b[:, 0:128]
        po = po_b[:, 0:128]
        pkv = pkv_b[:, 0:128]
        pt_slot = [ptr[:, 0:128], po_b[:, 384:448].bitcast(BF16)]
        kt_slot = [ptr[:, 0:128], ptr[:, 512:640]]

        sems = {}
        for s in ("pe", "dve", "act", "pool", "dg", "ds", "da"):
            sems[s] = stack.enter_context(nc.semaphore("sem_" + s))

        p = Prog(nc)
        p.sem = sems
        wm = WeightMgr(p, wf32, wbf16, wsl)
        tr_ev = [0, 0]
        tr_last = [0]

        vA = poolA[:]
        hdn = vA.rearrange("p (c f) -> p c f", c=64, f=512)
        slot = [vA[:, i * 8192:(i + 1) * 8192].rearrange("p (j f) -> p j f", j=NJ, f=D)
                for i in range(4)]
        grp = [poolB[:, i * 2048:(i + 1) * 2048].rearrange("p (j f) -> p j f", j=NJ, f=512)
               for i in range(5)]
        gate_tm = poolB[:, 0:8192].rearrange("p (j f) -> p j f", j=NJ, f=D)
        o2_tm = poolC[:].rearrange("p (j f) -> p j f", j=NJ, f=D)
        xconv = poolC[:].rearrange("p (c f) -> p c f", c=H, f=512)
        s_fm = poolC[:].rearrange("p (c f) -> p c f", c=H, f=512)
        y_bf = poolC[:].rearrange("p (j f) -> p j f", j=NJ, f=D)
        s_tm = poolD[:].rearrange("p (j f) -> p j f", j=NJ, f=D)
        g_fm = poolE[:].rearrange("p (c f) -> p c f", c=H, f=512)
        o2fm = g_fm
        omix_fm = g_fm
        xv = {}
        if x_in is not None:
            xv["x"] = x_in.rearrange("(n p) f -> n p f", p=128)
        xv["x1"] = x1.rearrange("(n p) f -> n p f", p=128)
        if out is not None:
            xv["out"] = out.rearrange("(n p) f -> n p f", p=128)

        def qd_rep(h):
            return bass.AP(qdt, h * 128, [[H * 128, 128], [0, NJ], [1, 128]])

        def junkD(off, width):
            return bass.AP(poolD, off, [[8192, 128], [1, width]])

        def dbg_dump(name, src_ap, wait_sem, it):
            if debug and name in dbg:
                p.emit("scalar", lambda e, it=it, s=src_ap: e.dma_start(out=dbg[name][it], in_=s),
                       waits=[(wait_sem, p.cnt[wait_sem]), ("da", p.cnt["da"])],
                       inc="da", inc_n=16, tag="dbg " + name)
                p.barrier()

        class _StopEmit(Exception):
            pass
        stage_i = [0]

        def gate():
            stage_i[0] += 1
            if stage_i[0] > KSTAGES:
                p.enabled = False

        for k, ap in cin.items():
            tgt = {"c_ddiag": dd_sb, "c_dqfull": dq_sb, "c_drev": drev_sb,
                   "c_cw": cw[:], "c_cb": cb[:], "c_gs": gsc[:], "c_us": usc[:], "c_ds": dsc[:]}.get(k)
            if k == "c_eye":
                p.dma("gpsimd", id_bf[:], ap, tag="eye")
            elif tgt is not None:
                p.dma("gpsimd", tgt, ap, tag=k)
        dg_consts = p.cnt["dg"]

        for buf in (kv_bf, s_state, halo, xn):
            p.emit("gpsimd", lambda e, b=buf: e.memset(b[:], 0.0), inc="pool", tag="memset")
        p.emit("gpsimd", lambda e: e.memset(onesf, 1.0), inc="pool", tag="ones")
        pool_init = p.cnt["pool"]

        gate()

        def bcast_row(src_ap, width, dst, tag):
            for b0 in range(0, width, 512):
                w = min(512, width - b0)
                nd = p.dma("gpsimd", rowbuf[0:1, 0:w], src_ap[0:1, b0:b0 + w],
                           waits=[("pe", p.cnt["pe"])], tag="row " + tag)
                nmm = p.emit("tensor",
                             lambda e, w=w: e.matmul(pb[0][:, 0:w], onesf[:],
                                                     rowbuf[:, 0:w], start=True, stop=True),
                             waits=[("dg", nd), ("pool", pool_init), ("dve", p.cnt["dve"])],
                             inc="pe", tag="bc mm " + tag)
                p.emit("vector",
                       lambda e, b0=b0, w=w, d=dst: e.tensor_copy(d[:, b0:b0 + w], pb[0][:, 0:w]),
                       waits=[("pe", nmm)], inc="dve", tag="bc ev " + tag)

        bcast_row(cin["c_slope"], H, rbc, "slope")
        gate()
        if 1 in layers:
            bcast_row(cin["c_gamma"], H, gbc, "gamma")
            bcast_row(cin["c_gbrow"], D, gb_tbl, "gb")
            bcast_row(cin["c_lnwrow"], D, lnw_tbl, "lnw")
            bcast_row(cin["c_lnbrow"], D, lnb_tbl, "lnb")

        gate()
        if 0 in layers:
            c0 = p.snap()
            n1 = p.emit("vector", lambda e: e.tensor_scalar(stat[:, 0:H], rbc[:], -128.0, None, ALU.mult),
                        waits=[("dve", c0["dve"])], inc="dve", tag="bd pre")
            p.emit("scalar", lambda e: e.activation(bdc[:], stat[:, 0:H], AF.Exp),
                   waits=[("dve", n1)], inc="act", tag="bd exp")
            drev_b = bass.AP(scrA, 256, [[512, 128], [0, H]])
            n2 = p.emit("vector", lambda e: e.tensor_tensor(stat[:, H:2 * H], rbc[:], drev_b, ALU.mult),
                        waits=[("dve", c0["dve"]), ("dg", dg_consts)], inc="dve", tag="kd pre")
            p.emit("scalar", lambda e: e.activation(kdc[:], stat[:, H:2 * H], AF.Exp, scale=-1.0),
                   waits=[("dve", n2)], inc="act", tag="kd exp")
            for h in range(H):
                nm = p.emit("vector",
                            lambda e, h=h: e.tensor_scalar(tmp128, dd_sb, rbc[:, h:h + 1], None, ALU.mult),
                            waits=[("dg", dg_consts), ("act", p.cnt["act"])], inc="dve", tag=f"mpre {h}")
                p.emit("scalar", lambda e, h=h: e.activation(masks[:, h, :], tmp128, AF.Exp, scale=-1.0),
                       waits=[("dve", nm)], inc="act", tag=f"mexp {h}")
                nq = p.emit("vector",
                            lambda e, h=h: e.tensor_scalar(tmp128, dq_sb, rbc[:, h:h + 1], None, ALU.mult),
                            waits=[("act", p.cnt["act"])], inc="dve", tag=f"qpre {h}")
                p.emit("scalar", lambda e, h=h: e.activation(
                    bass.AP(qdt, h * 128, [[H * 128, 128], [1, 128]]), tmp128, AF.Exp, scale=-1.0),
                       waits=[("dve", nq)], inc="act", tag=f"qexp {h}")

        gate()
        p.barrier()

        bank_free = {id(b): {} for b in pb}

        def bfree(bank, **kw):
            d = bank_free[id(bank)]
            for k, v in kw.items():
                d[k] = max(d.get(k, 0), v)

        def ln_stage(it, src):
            en = p.enabled
            for jj in range(NJ):
                c0 = p.snap()
                nl = p.dma("gpsimd", xst0[:], xv[src][it * NJ + jj],
                           waits=[("dve", c0["dve"]), ("da", c0["da"])],
                           tag=f"ln ld {jj}")
                if KLNPARTS < 2:
                    p.enabled = False
                na = p.emit("scalar",
                            lambda e, jj=jj: e.activation(junkD(jj * 2048, 2048), xst0[:], AF.Square,
                                                          accum_out=stat[:, jj:jj + 1]),
                            waits=[("dg", nl), ("dve", c0["dve"])], inc="act", tag=f"ln sq {jj}")
                ns = p.emit("vector", lambda e, jj=jj: e.reduce_sum(stat[:, 8 + jj:9 + jj], xst0[:], axis=AX.X),
                            waits=[("dg", nl)], inc="dve", tag=f"ln sum {jj}")
                if KLNPARTS < 3:
                    p.enabled = False
                nv1 = p.emit("vector", lambda e, jj=jj: e.tensor_scalar(
                    stat[:, 16 + jj:17 + jj], stat[:, 8 + jj:9 + jj], 1.0 / D, None, ALU.mult),
                             waits=[("dve", ns)], inc="dve", tag="ln mean")
                nv2 = p.emit("vector", lambda e, jj=jj: e.tensor_tensor(
                    stat[:, 24 + jj:25 + jj], stat[:, 16 + jj:17 + jj], stat[:, 16 + jj:17 + jj], ALU.mult),
                             waits=[("dve", nv1)], inc="dve", tag="ln m2")
                nv3 = p.emit("vector", lambda e, jj=jj: e.scalar_tensor_tensor(
                    stat[:, 32 + jj:33 + jj], stat[:, jj:jj + 1], 1.0 / D, stat[:, 24 + jj:25 + jj],
                    ALU.mult, ALU.subtract),
                             waits=[("act", na), ("dve", nv2)], inc="dve", tag="ln var")
                nv4 = p.emit("vector", lambda e, jj=jj: e.tensor_scalar(
                    stat[:, 40 + jj:41 + jj], stat[:, 32 + jj:33 + jj], 1e-5, None, ALU.add),
                             waits=[("dve", nv3)], inc="dve", tag="ln eps")
                nv5 = p.emit("vector", lambda e, jj=jj: e.reciprocal(stat[:, 48 + jj:49 + jj], stat[:, 40 + jj:41 + jj]),
                             waits=[("dve", nv4)], inc="dve", tag="ln recip")
                na2 = p.emit("scalar", lambda e, jj=jj: e.activation(
                    stat[:, 56 + jj:57 + jj], stat[:, 48 + jj:49 + jj], AF.Sqrt),
                             waits=[("dve", nv5)], inc="act", tag="ln sqrt")
                if KLNPARTS < 4:
                    p.enabled = False
                nap = p.emit("vector",
                             lambda e, jj=jj: e.tensor_scalar(s_tm[:, jj, :], xst0[:],
                                                              stat[:, 16 + jj:17 + jj], stat[:, 56 + jj:57 + jj],
                                                              ALU.subtract, ALU.mult),
                             waits=[("act", na2)], inc="dve", tag=f"ln apply {jj}")
                if KLNPARTS < 5:
                    p.enabled = False
                for c in range(H):
                    sl = (jj * H + c) % 2
                    ntr = p.emit("tensor",
                                 lambda e, jj=jj, c=c, sl=sl: e.transpose(
                                     pt_slot[sl], s_tm[:, jj, c * 128:(c + 1) * 128], id_bf[:]),
                                 waits=[("dve", nap), ("dve", tr_ev[sl])],
                                 inc="pe", tag=f"lntr {jj}.{c}")
                    tr_ev[sl] = p.emit("vector",
                                       lambda e, jj=jj, c=c, sl=sl: e.tensor_copy(
                                           xn[:, c, 3 + jj * 128:3 + (jj + 1) * 128], pt_slot[sl]),
                                       waits=[("pe", ntr)], inc="dve", tag=f"lnev {jj}.{c}")
                p.enabled = en

        def transpose_set(get_src, get_dst, n_c, n_j, tag, waits_extra=()):
            for c in range(n_c):
                for jj in range(n_j):
                    sl = (c * n_j + jj) % 2
                    s_ap = get_src(c, jj)
                    d_ap = get_dst(c, jj)
                    ntr = p.emit("tensor",
                                 lambda e, s_ap=s_ap, sl=sl: e.transpose(pt_slot[sl], s_ap, id_bf[:]),
                                 waits=list(waits_extra) + [("dve", tr_ev[sl])],
                                 inc="pe", tag=f"tr {tag} {c}.{jj}")
                    tr_ev[sl] = p.emit("vector",
                                       lambda e, d_ap=d_ap, sl=sl: e.tensor_copy(d_ap, pt_slot[sl]),
                                       waits=[("pe", ntr)], inc="dve", tag=f"trev {tag}")

        def proj_tm(w_name, s_range, lhs3, lhs_off, out_cb, pass0, tag, lhs_waits):
            for s in s_range:
                view, wws, slotid, extra = wm.load(w_name, s, pass0)
                last_pe = 0
                for jj in range(NJ):
                    bank = pb[jj % 2 + (2 if (s % 2) else 0)]
                    def mmf(e, view=view, jj=jj, bank=bank):
                        r = None
                        for fi in range(H):
                            r = e.matmul(bank[:, 0:256], lhs3[:, fi, lhs_off + jj * 128:lhs_off + (jj + 1) * 128],
                                         view[:, fi, :], start=(fi == 0), stop=(fi == H - 1))
                        return r
                    nmm = p.emit("tensor", mmf,
                                 waits=wws + list(lhs_waits) + list(bank_free[id(bank)].items()),
                                 inc="pe", tag=f"{tag} mm s{s} j{jj}")
                    last_pe = nmm
                    out_cb(jj, s, bank, nmm)
                wm.release(slotid, [("pe", last_pe)], extra)

        def resid_stage(it, src, dst, g_ready_act, s_ready_dve):
            nonlocal xst1_store
            prev_last = 0
            for jj in range(NJ):
                nl = p.dma("gpsimd", xst0[:], xv[src][it * NJ + jj],
                           waits=[("dve", p.cnt["dve"]), ("da", p.cnt["da"])], tag=f"res ld {jj}")
                chunk = [prev_last]
                for c in range(H):
                    sl = c % 2
                    ntr = p.emit("tensor",
                                 lambda e, c=c, jj=jj, sl=sl: e.transpose(
                                     pt_slot[sl], g_fm[:, c, jj * 128:(jj + 1) * 128], id_bf[:]),
                                 waits=[("act", g_ready_act), ("dve", tr_ev[sl])],
                                 inc="pe", tag=f"gtr {jj}.{c}")
                    nsg = p.emit("vector",
                                 lambda e, c=c, jj=jj, sl=sl: e.tensor_tensor(
                                     sgt, pt_slot[sl], s_tm[:, jj, c * 128:(c + 1) * 128], ALU.mult),
                                 waits=[("pe", ntr), ("dve", s_ready_dve), ("dve", chunk[-1])],
                                 inc="dve", tag=f"sg {jj}.{c}")
                    tr_ev[sl] = nsg
                    nxo = p.emit("vector",
                                 lambda e, c=c: e.tensor_tensor(
                                     xst0[:, c * 128:(c + 1) * 128], xst0[:, c * 128:(c + 1) * 128],
                                     sgt, ALU.add),
                                 waits=[("dve", nsg), ("dg", nl)],
                                 inc="dve", tag=f"xo {jj}.{c}")
                    chunk.append(nxo)
                prev_last = chunk[-1]
                xst1_store = p.emit("scalar",
                                    lambda e, jj=jj, dst=dst, it=it: e.dma_start(out=xv[dst][it * NJ + jj], in_=xst0[:]),
                                    waits=[("dve", max(chunk)), ("da", p.cnt["da"])],
                                    inc="da", inc_n=16, tag=f"res st {jj}")

        xst1_store = 0

        kvbf_cnt = {h: 0 for h in range(H)}

        def _tile_loop_body():
          nonlocal xst1_store
          for it in range(KTILES):
            pass0 = (it == 0)
            if 0 in layers:
                gate()
                ln_stage(it, "x")
                xn_done = p.cnt["dve"]
                dbg_dump("xn0", xn[:].rearrange("p a b -> p (a b)"), "dve", it)

                gate()
                q_sil, qqd, k_sil, v_tm, u_tm = grp
                for g in range(NG):
                    def proj_fm(w_name, slabs, out_cb, tag):
                        last = 0
                        for si, s in enumerate(slabs):
                            view, wws, slotid, extra = wm.load(w_name, s, pass0)
                            for j in range(2):
                                jloc = 2 * si + j
                                bank = pb[jloc % 2]
                                def mmf(e, view=view, j=j, bank=bank):
                                    r = None
                                    for fi in range(H):
                                        r = e.matmul(bank[:], view[:, fi, j * 128:(j + 1) * 128],
                                                     xn[:, fi, 3:515], start=(fi == 0), stop=(fi == H - 1))
                                    return r
                                nmm = p.emit("tensor", mmf,
                                             waits=wws + [("dve", xn_done)] + list(bank_free[id(bank)].items()),
                                             inc="pe", tag=f"{tag} mm {jloc}")
                                last = nmm
                                out_cb(jloc, bank, nmm)
                            wm.release(slotid, [("pe", last)], extra)

                    def q_out(jloc, bank, nmm, g=g):
                        h = 4 * g + jloc
                        na = p.emit("scalar",
                                    lambda e, jloc=jloc, bank=bank: e.activation(q_sil[:, jloc, :], bank[:], AF.Silu),
                                    waits=[("pe", nmm)], inc="act", tag=f"q sil {h}")
                        p.emit("vector",
                               lambda e, jloc=jloc, h=h: e.tensor_tensor(
                                   qqd[:, jloc, :].rearrange("p (r f) -> p r f", r=NJ),
                                   q_sil[:, jloc, :].rearrange("p (r f) -> p r f", r=NJ),
                                   qd_rep(h), ALU.mult),
                               waits=[("act", na)], inc="dve", tag=f"qqd {h}")
                        bfree(bank, act=na)
                    proj_fm("wqkvu", [2 * g, 2 * g + 1], q_out, f"q g{g}")

                    def k_out(jloc, bank, nmm, g=g):
                        na = p.emit("scalar",
                                    lambda e, jloc=jloc, bank=bank: e.activation(k_sil[:, jloc, :], bank[:], AF.Silu),
                                    waits=[("pe", nmm)], inc="act", tag=f"k sil g{g}.{jloc}")
                        bfree(bank, act=na)
                    proj_fm("wqkvu", [8 + 2 * g, 9 + 2 * g], k_out, f"k g{g}")

                    def v_out(jj, s, bank, nmm, g=g):
                        b0 = (s - 16 - 2 * g) * 256
                        nv = p.emit("vector",
                                    lambda e, jj=jj, b0=b0, bank=bank: e.tensor_copy(
                                        v_tm[:, jj, b0:b0 + 256], bank[:, 0:256]),
                                    waits=[("pe", nmm)], inc="dve", tag=f"v ev {jj}")
                        bfree(bank, dve=nv)
                    proj_tm("wqkvu", [16 + 2 * g, 17 + 2 * g], xn, 3, v_out, pass0, f"v g{g}", [("dve", xn_done)])

                    def u_out(jj, s, bank, nmm, g=g):
                        b0 = (s - 24 - 2 * g) * 256
                        nv = p.emit("vector",
                                    lambda e, jj=jj, b0=b0, bank=bank: e.tensor_copy(
                                        u_tm[:, jj, b0:b0 + 256], bank[:, 0:256]),
                                    waits=[("pe", nmm)], inc="dve", tag=f"u ev {jj}")
                        bfree(bank, dve=nv)
                    proj_tm("wqkvu", [24 + 2 * g, 25 + 2 * g], xn, 3, u_out, pass0, f"u g{g}", [("dve", xn_done)])

                    grp_ready = p.snap()
                    po_free = {"act": 0, "dve": 0}
                    pqk_free = 0
                    pkv_free = 0
                    hsl = [hstage[:, 0:128], hstage[:, 128:256]]
                    hsl_free = [dict(), dict()]
                    en2 = p.enabled
                    for blk in range(NJ):
                        for hl in range(4):
                            h = 4 * g + hl
                            sl = (blk * 4 + hl) % 2
                            if KT2PARTS < 2:
                                p.enabled = False
                            ntr = p.emit("tensor",
                                         lambda e, hl=hl, blk=blk, sl=sl: e.transpose(
                                             kt_slot[sl], k_sil[:, hl, blk * 128:(blk + 1) * 128], id_bf[:]),
                                         waits=[("act", grp_ready["act"]), ("dve", tr_ev[0]), ("dve", tr_ev[1])],
                                         inc="pe", tag=f"ktr {h}.{blk}")
                            nkd = p.emit("vector",
                                         lambda e, h=h, sl=sl: e.tensor_scalar(
                                             ktk[sl], kt_slot[sl], kdc[:, h:h + 1], None, ALU.mult),
                                         waits=[("pe", ntr)], inc="dve", tag=f"ktk {h}.{blk}")
                            tr_ev[sl] = nkd
                            if KT2PARTS < 3:
                                p.enabled = False
                            nqk = p.emit("tensor",
                                         lambda e, hl=hl, blk=blk: e.matmul(
                                             pqk, k_sil[:, hl, blk * 128:(blk + 1) * 128],
                                             q_sil[:, hl, blk * 128:(blk + 1) * 128], start=True, stop=True),
                                         waits=[("dve", pqk_free)], inc="pe", tag=f"qk {h}.{blk}")
                            nmsk = p.emit("vector",
                                          lambda e, h=h, sl=sl: e.scalar_tensor_tensor(
                                              qkm[sl], masks[:, h, :], 1.0, pqk, ALU.bypass, ALU.mult),
                                          waits=[("pe", nqk)], inc="dve", tag=f"mask {h}.{blk}")
                            pqk_free = nmsk
                            if KT2PARTS < 4:
                                p.enabled = False
                            def o_mm(e, hl=hl, blk=blk, h=h, sl=sl):
                                e.matmul(po, qqd[:, hl, blk * 128:(blk + 1) * 128],
                                         kv_bf[:, h * 128:(h + 1) * 128], start=True, stop=False)
                                return e.matmul(po, qkm[sl], v_tm[:, blk, hl * 128:(hl + 1) * 128],
                                                start=False, stop=True)
                            nomm = p.emit("tensor", o_mm,
                                          waits=[("dve", nmsk), ("dve", kvbf_cnt[h]),
                                                 ("act", po_free["act"]), ("dve", po_free["dve"])],
                                          inc="pe", tag=f"o {h}.{blk}")
                            if KT2PARTS < 5:
                                p.enabled = False
                            nkvm = p.emit("tensor",
                                          lambda e, blk=blk, hl=hl, sl=sl: e.matmul(
                                              pkv, ktk[sl], v_tm[:, blk, hl * 128:(hl + 1) * 128],
                                              start=True, stop=True),
                                          waits=[("dve", nkd), ("dve", pkv_free)], inc="pe", tag=f"kvmm {h}.{blk}")
                            if KT2PARTS < 6:
                                p.enabled = False
                            sl2 = (blk * 4 + hl) % 2
                            ncp = p.emit("vector",
                                         lambda e, sl2=sl2: e.tensor_copy(hsl[sl2], po),
                                         waits=[("pe", nomm)] + list(hsl_free[sl2].items()),
                                         inc="dve", tag=f"oev {h}.{blk}")
                            no2 = p.emit("vector",
                                         lambda e, blk=blk, hl=hl, h=h, sl2=sl2: e.tensor_tensor(
                                             o2_tm[:, blk, h * 128:(h + 1) * 128],
                                             hsl[sl2],
                                             u_tm[:, blk, hl * 128:(hl + 1) * 128], ALU.mult),
                                         waits=[("dve", ncp)], inc="dve", tag=f"o2 {h}.{blk}")
                            nss = p.emit("scalar",
                                         lambda e, blk=blk, h=h, sl2=sl2: e.activation(
                                             junkD(blk * 2048 + ((h * 128) % 2048), 128),
                                             hsl[sl2], AF.Square, accum_out=ssc[:, blk, h:h + 1]),
                                         waits=[("dve", ncp)], inc="act", tag=f"ss {h}.{blk}")
                            hsl_free[sl2] = {"act": nss, "dve": no2}
                            po_free = {"dve": ncp, "act": 0}
                            if KT2PARTS < 7:
                                p.enabled = False
                            nkv1 = p.emit("vector",
                                          lambda e, h=h: e.scalar_tensor_tensor(
                                              kv_bf[:, h * 128:(h + 1) * 128], kv_bf[:, h * 128:(h + 1) * 128],
                                              bdc[:, h:h + 1], pkv, ALU.mult, ALU.add),
                                          waits=[("pe", nkvm)], inc="dve", tag=f"kvup {h}.{blk}")
                            pkv_free = nkv1
                            kvbf_cnt[h] = nkv1
                            p.enabled = en2

                gate()
                c1 = p.snap()
                nred = p.emit("vector", lambda e: e.reduce_sum(stat[:, 0:NJ], ssc[:], axis=AX.X),
                              waits=[("act", c1["act"])], inc="dve", tag="rms red")
                nms = p.emit("vector", lambda e: e.tensor_scalar(stat[:, 8:8 + NJ], stat[:, 0:NJ],
                                                                 1.0 / D, 1e-6, ALU.mult, ALU.add),
                             waits=[("dve", nred)], inc="dve", tag="rms ms")
                nrc = p.emit("vector", lambda e: e.reciprocal(stat[:, 16:16 + NJ], stat[:, 8:8 + NJ]),
                             waits=[("dve", nms)], inc="dve", tag="rms recip")
                nsq = p.emit("scalar", lambda e: e.activation(stat[:, 24:24 + NJ], stat[:, 16:16 + NJ], AF.Sqrt),
                             waits=[("dve", nrc)], inc="act", tag="rms sqrt")
                o2s = []
                for blk in range(NJ):
                    o2s.append(p.emit("vector",
                                      lambda e, blk=blk: e.tensor_scalar(
                                          o2_tm[:, blk, :], o2_tm[:, blk, :],
                                          stat[:, 24 + blk:25 + blk], None, ALU.mult),
                                      waits=[("act", nsq), ("dve", c1["dve"])], inc="dve", tag=f"o2s {blk}"))
                dbg_dump("o2", poolC[:], "dve", it)

                transpose_set(lambda c, jj: o2_tm[:, jj, c * 128:(c + 1) * 128],
                              lambda c, jj: o2fm[:, c, jj * 128:(jj + 1) * 128],
                              H, NJ, "o2", waits_extra=[("dve", max(o2s))])
                o2fm_done = p.cnt["dve"]

                def wout_out(jj, s, bank, nmm):
                    nv = p.emit("vector",
                                lambda e, jj=jj, s=s, bank=bank: e.tensor_copy(
                                    s_tm[:, jj, s * 256:(s + 1) * 256], bank[:, 0:256]),
                                waits=[("pe", nmm)], inc="dve", tag=f"s ev {jj}.{s}")
                    bfree(bank, dve=nv)
                proj_tm("wo0", range(8), o2fm, 0, wout_out, pass0, "wo0", [("dve", o2fm_done)])
                s0_done = p.cnt["dve"]
                dbg_dump("s0", poolD[:], "dve", it)

                gate()
                prev_hdn = 0
                for cs in range(44):
                    view, wws, slotid, extra = wm.load("wgu0", cs, pass0)
                    fo = cs
                    bg, bu = pb[2 * (cs % 2)], pb[2 * (cs % 2) + 1]
                    def gmm(e, view=view, bg=bg):
                        r = None
                        for fi in range(H):
                            r = e.matmul(bg[:], view[:, fi, 0:128], xn[:, fi, 3:515],
                                         start=(fi == 0), stop=(fi == H - 1))
                        return r
                    ng = p.emit("tensor", gmm,
                                waits=wws + [("dve", xn_done)] + list(bank_free[id(bg)].items()),
                                inc="pe", tag=f"f0g {fo}")
                    def umm(e, view=view, bu=bu):
                        r = None
                        for fi in range(H):
                            r = e.matmul(bu[:], view[:, fi, 128:256],
                                         xn[:, fi, 3:515], start=(fi == 0), stop=(fi == H - 1))
                        return r
                    nu = p.emit("tensor", umm,
                                waits=wws + list(bank_free[id(bu)].items()),
                                inc="pe", tag=f"f0u {fo}")
                    na = p.emit("scalar",
                                lambda e, fo=fo, bg=bg: e.activation(sil_scr[:], bg[:], AF.Silu,
                                                                     scale=gsc[:, fo:fo + 1]),
                                waits=[("pe", ng), ("dve", prev_hdn)], inc="act", tag=f"f0sil {fo}")
                    prev_hdn = p.emit("vector",
                                      lambda e, fo=fo, bu=bu: e.scalar_tensor_tensor(
                                          hdn[:, fo, :], bu[:], usc[:, fo:fo + 1], sil_scr[:],
                                          ALU.mult, ALU.mult),
                                      waits=[("act", na), ("pe", nu)], inc="dve", tag=f"f0hdn {fo}")
                    bfree(bg, act=na)
                    bfree(bu, dve=prev_hdn)
                    wm.release(slotid, [("pe", nu)], extra)
                hdn_all = p.cnt["dve"]

                gate()
                g0_last = 0
                for s in range(16):
                    bank = pb[s % 2]
                    nmm = 0
                    for half in range(2):
                        view, wws, slotid, extra = wm.load("wd0", 2 * s + half, pass0)
                        def dmm(e, view=view, bank=bank, half=half):
                            r = None
                            for c in range(22):
                                r = e.matmul(bank[:], view[:, c, :], hdn[:, 22 * half + c, :],
                                             start=(half == 0 and c == 0), stop=(half == 1 and c == 21))
                            return r
                        nmm = p.emit("tensor", dmm,
                                     waits=wws + [("dve", hdn_all)] + list(bank_free[id(bank)].items()),
                                     inc="pe", tag=f"d0 {s}.{half}")
                        wm.release(slotid, [("pe", nmm)], extra)
                    g0_last = p.emit("scalar",
                                     lambda e, s=s, bank=bank: e.activation(g_fm[:, s, :], bank[:], AF.Silu,
                                                                            scale=dsc[:, s:s + 1]),
                                     waits=[("pe", nmm)], inc="act", tag=f"d0sil {s}")
                    bfree(bank, act=g0_last)
                dbg_dump("g0", poolE[:], "act", it)

                gate()
                resid_stage(it, "x", "x1", g0_last, s0_done)

            if 1 in layers:
                gate()
                cpre = p.snap()
                p.emit("vector", lambda e: e.tensor_copy(xn[:, :, 0:3], halo[:]),
                       waits=[("pe", cpre["pe"]), ("pool", p.cnt["pool"])],
                       inc="dve", tag="halo in")
                ln_stage(it, "x1")
                p.emit("vector", lambda e: e.tensor_copy(halo[:], xn[:, :, 512:515]),
                       waits=[("dve", p.cnt["dve"])], inc="dve", tag="halo out")
                xn_done = p.cnt["dve"]
                dbg_dump("xn1", xn[:].rearrange("p a b -> p (a b)"), "dve", it)

                gate()
                prev_sil = 0
                for c in range(H):
                    n1 = p.emit("vector",
                                lambda e, c=c: e.tensor_scalar(scrA[:, 0:512], xn[:, c, 0:512],
                                                               cw[:, c, 0:1], None, ALU.mult),
                                waits=[("dve", xn_done), ("act", prev_sil)], inc="dve", tag=f"cv0 {c}")
                    for k in range(1, 4):
                        n1 = p.emit("vector",
                                    lambda e, c=c, k=k: e.scalar_tensor_tensor(
                                        scrA[:, 0:512], xn[:, c, k:k + 512], cw[:, c, k:k + 1],
                                        scrA[:, 0:512], ALU.mult, ALU.add),
                                    waits=[("dve", n1)], inc="dve", tag=f"cv{k} {c}")
                    prev_sil = p.emit("scalar",
                                      lambda e, c=c: e.activation(xconv[:, c, :], scrA[:, 0:512], AF.Silu,
                                                                  bias=cb[:, c:c + 1]),
                                      waits=[("dve", n1)], inc="act", tag=f"cvsil {c}")
                xconv_done = prev_sil
                dbg_dump("xconv", poolC[:], "act", it)

                gate()
                k_n, v_b, kv_tm, q_n = slot

                def norm_proj(w_name, dst, tag):
                    red_cnt = [0]
                    def cb_(jj, s, bank, nmm, dst=dst):
                        nv = p.emit("vector",
                                    lambda e, jj=jj, s=s, bank=bank: e.tensor_copy(
                                        dst[:, jj, s * 256:(s + 1) * 256], bank[:, 0:256]),
                                    waits=[("pe", nmm)], inc="dve", tag=f"{tag} ev {jj}.{s}")
                        na = p.emit("scalar",
                                    lambda e, bank=bank: e.activation(scrB[:, 0:256], bank[:, 0:256], AF.Square),
                                    waits=[("pe", nmm), ("dve", red_cnt[-1]), ("dve", nv)],
                                    inc="act", tag=f"{tag} sq")
                        nr = p.emit("vector",
                                    lambda e, jj=jj, s=s: e.reduce_sum(
                                        nrm[:, jj * H + s * 2:jj * H + s * 2 + 2],
                                        scrB[:, 0:256].rearrange("p (a b) -> p a b", a=2), axis=AX.X),
                                    waits=[("act", na)], inc="dve", tag=f"{tag} red")
                        red_cnt.append(nr)
                        bfree(bank, dve=nv, act=na)
                    proj_tm(w_name, range(8), xn, 3, cb_, pass0, tag, [("dve", xn_done)])
                    nq1 = p.emit("scalar", lambda e: e.activation(stat[:, 0:NJ * H], nrm[:], AF.Sqrt),
                                 waits=[("dve", red_cnt[-1])], inc="act", tag=f"{tag} nsq")
                    nq2 = p.emit("vector", lambda e: e.tensor_scalar(stat[:, 0:NJ * H], stat[:, 0:NJ * H],
                                                                     1e-12, None, ALU.max),
                                 waits=[("act", nq1)], inc="dve", tag=f"{tag} nmax")
                    nq3 = p.emit("vector", lambda e: e.reciprocal(nrm[:], stat[:, 0:NJ * H]),
                                 waits=[("dve", nq2)], inc="dve", tag=f"{tag} nrec")
                    last = 0
                    for jj in range(NJ):
                        for h in range(H):
                            last = p.emit("vector",
                                          lambda e, jj=jj, h=h, dst=dst: e.tensor_scalar(
                                              dst[:, jj, h * 128:(h + 1) * 128],
                                              dst[:, jj, h * 128:(h + 1) * 128],
                                              nrm[:, jj * H + h:jj * H + h + 1], None, ALU.mult),
                                          waits=[("dve", nq3)], inc="dve", tag=f"{tag} ap {jj}.{h}")
                    return last

                k_done = norm_proj("wk1", k_n, "k1")
                def v_out1(jj, s, bank, nmm):
                    nv = p.emit("vector",
                                lambda e, jj=jj, s=s, bank=bank: e.tensor_copy(
                                    v_b[:, jj, s * 256:(s + 1) * 256], bank[:, 0:256]),
                                waits=[("pe", nmm)], inc="dve", tag=f"v1 ev {jj}.{s}")
                    bfree(bank, dve=nv)
                proj_tm("wv1", range(8), xn, 3, v_out1, pass0, "v1", [("dve", xn_done)])
                v_done = p.cnt["dve"]
                kv_done = []
                for jj in range(NJ):
                    kv_done.append(p.emit("vector",
                                          lambda e, jj=jj: e.tensor_tensor(
                                              kv_tm[:, jj, :], k_n[:, jj, :], v_b[:, jj, :], ALU.mult),
                                          waits=[("dve", max(k_done, v_done))], inc="dve", tag=f"kv1 {jj}"))
                dbg_dump("kvtm", poolA[:, 16384:24576], "dve", it)
                q_done = norm_proj("wq1", q_n, "q1")

                def gate_out(jj, s, bank, nmm):
                    nv = p.emit("vector",
                                lambda e, jj=jj, s=s, bank=bank: e.scalar_tensor_tensor(
                                    scrC[:, 0:256], gb_tbl[:, s * 256:(s + 1) * 256], 1.0, bank[:, 0:256],
                                    ALU.bypass, ALU.add),
                                waits=[("pe", nmm), ("act", p.cnt["act"])], inc="dve", tag=f"ga {jj}.{s}")
                    na = p.emit("scalar",
                                lambda e, jj=jj, s=s: e.activation(
                                    gate_tm[:, jj, s * 256:(s + 1) * 256], scrC[:, 0:256], AF.Sigmoid),
                                waits=[("dve", nv)], inc="act", tag=f"gs {jj}.{s}")
                    bfree(bank, dve=nv)
                proj_tm("wgt1", range(8), xconv, 0, gate_out, pass0, "gt1", [("act", xconv_done)])
                gate_done = p.cnt["act"]

                gate()
                scol = []
                for c in range(H):
                    evs = []
                    for jj in range(NJ):
                        sl = (c * NJ + jj) % 2
                        ntr = p.emit("tensor",
                                     lambda e, c=c, jj=jj, sl=sl: e.transpose(
                                         pt_slot[sl], kv_tm[:, jj, c * 128:(c + 1) * 128], id_bf[:]),
                                     waits=[("dve", max(kv_done)), ("dve", tr_ev[sl])],
                                     inc="pe", tag=f"kvtr {c}.{jj}")
                        ne = p.emit("vector",
                                    lambda e, jj=jj, sl=sl: e.tensor_copy(
                                        scrB[:, jj * 128:(jj + 1) * 128], pt_slot[sl]),
                                    waits=[("pe", ntr), ("act", gate_done)], inc="dve", tag=f"kvev {c}.{jj}")
                        tr_ev[sl] = ne
                        evs.append(ne)
                    g_ap = bass.AP(gbc, c, [[H, 128], [0, 512]])
                    nsc = p.emit("vector",
                                 lambda e, c=c, g_ap=g_ap: e.tensor_tensor_scan(
                                     scrC[:], g_ap, scrB[:], s_state[:, c:c + 1], ALU.mult, ALU.add),
                                 waits=[("dve", max(evs))], inc="dve", tag=f"scan {c}")
                    p.emit("vector", lambda e, c=c: e.tensor_copy(s_state[:, c:c + 1], scrC[:, 511:512]),
                           waits=[("dve", nsc)], inc="dve", tag=f"scan st {c}")
                    scol.append(p.emit("vector", lambda e, c=c: e.tensor_copy(s_fm[:, c, :], scrC[:]),
                                       waits=[("dve", nsc)], inc="dve", tag=f"scan cp {c}"))
                dbg_dump("sfm", poolC[:], "dve", it)

                transpose_set(lambda c, jj: s_fm[:, c, jj * 128:(jj + 1) * 128],
                              lambda c, jj: v_b[:, jj, c * 128:(c + 1) * 128],
                              H, NJ, "sT", waits_extra=[("dve", max(scol))])
                stm_done = p.cnt["dve"]
                omx = []
                for jj in range(NJ):
                    n1 = p.emit("vector",
                                lambda e, jj=jj: e.tensor_tensor(
                                    k_n[:, jj, :], v_b[:, jj, :], gate_tm[:, jj, :], ALU.mult),
                                waits=[("dve", stm_done)], inc="dve", tag=f"om1 {jj}")
                    omx.append(p.emit("vector",
                                      lambda e, jj=jj: e.tensor_tensor(
                                          kv_tm[:, jj, :], k_n[:, jj, :], q_n[:, jj, :], ALU.mult),
                                      waits=[("dve", n1), ("dve", q_done)], inc="dve", tag=f"om2 {jj}"))
                dbg_dump("omix", poolA[:, 16384:24576], "dve", it)
                transpose_set(lambda c, jj: kv_tm[:, jj, c * 128:(c + 1) * 128],
                              lambda c, jj: omix_fm[:, c, jj * 128:(jj + 1) * 128],
                              H, NJ, "om", waits_extra=[("dve", max(omx))])
                omixfm_done = p.cnt["dve"]

                gate()
                ybf = {}
                def y_out(jj, s, bank, nmm):
                    nv = p.emit("vector",
                                lambda e, jj=jj, s=s, bank=bank: e.tensor_copy(
                                    y_bf[:, jj, s * 256:(s + 1) * 256], bank[:, 0:256]),
                                waits=[("pe", nmm)], inc="dve", tag=f"y ev {jj}.{s}")
                    ybf[(jj, s)] = nv
                    bfree(bank, dve=nv)
                proj_tm("wo1", range(8), omix_fm, 0, y_out, pass0, "wo1", [("dve", omixfm_done)])
                prev_m3 = 0
                for jj in range(NJ):
                    w0 = max(ybf[(jj, s)] for s in range(8))
                    na = p.emit("scalar",
                                lambda e, jj=jj: e.activation(junkD(jj * 2048, 2048), y_bf[:, jj, :],
                                                              AF.Square, accum_out=stat[:, jj:jj + 1]),
                                waits=[("dve", w0)], inc="act", tag=f"y sq {jj}")
                    ns = p.emit("vector", lambda e, jj=jj: e.reduce_sum(stat[:, 8 + jj:9 + jj],
                                                                        y_bf[:, jj, :], axis=AX.X),
                                waits=[("dve", w0)], inc="dve", tag=f"y sum {jj}")
                    nv1 = p.emit("vector", lambda e, jj=jj: e.tensor_scalar(
                        stat[:, 16 + jj:17 + jj], stat[:, 8 + jj:9 + jj], 1.0 / D, None, ALU.mult),
                                 waits=[("dve", ns)], inc="dve", tag="y mean")
                    nv2 = p.emit("vector", lambda e, jj=jj: e.tensor_tensor(
                        stat[:, 24 + jj:25 + jj], stat[:, 16 + jj:17 + jj],
                        stat[:, 16 + jj:17 + jj], ALU.mult),
                                 waits=[("dve", nv1)], inc="dve", tag="y m2")
                    nv3 = p.emit("vector", lambda e, jj=jj: e.scalar_tensor_tensor(
                        stat[:, 32 + jj:33 + jj], stat[:, jj:jj + 1], 1.0 / D,
                        stat[:, 24 + jj:25 + jj], ALU.mult, ALU.subtract),
                                 waits=[("act", na), ("dve", nv2)], inc="dve", tag="y var")
                    nv4 = p.emit("vector", lambda e, jj=jj: e.tensor_scalar(
                        stat[:, 40 + jj:41 + jj], stat[:, 32 + jj:33 + jj], 1e-5, None, ALU.add),
                                 waits=[("dve", nv3)], inc="dve", tag="y eps")
                    nv5 = p.emit("vector", lambda e, jj=jj: e.reciprocal(
                        stat[:, 48 + jj:49 + jj], stat[:, 40 + jj:41 + jj]),
                                 waits=[("dve", nv4)], inc="dve", tag="y recip")
                    na2 = p.emit("scalar", lambda e, jj=jj: e.activation(
                        stat[:, 56 + jj:57 + jj], stat[:, 48 + jj:49 + jj], AF.Sqrt),
                                 waits=[("dve", nv5)], inc="act", tag="y sqrt")
                    for q4 in range(4):
                        o = q4 * 512
                        m1 = p.emit("vector",
                                    lambda e, jj=jj, o=o: e.tensor_scalar(
                                        scrA[:], y_bf[:, jj, o:o + 512],
                                        stat[:, 16 + jj:17 + jj], stat[:, 56 + jj:57 + jj],
                                        ALU.subtract, ALU.mult),
                                    waits=[("act", na2), ("dve", prev_m3)], inc="dve", tag=f"y ap1 {jj}.{q4}")
                        m2 = p.emit("vector",
                                    lambda e, o=o: e.scalar_tensor_tensor(
                                        scrA[:], lnw_tbl[:, o:o + 512], 1.0, scrA[:], ALU.bypass, ALU.mult),
                                    waits=[("dve", m1)], inc="dve", tag=f"y ap2 {jj}.{q4}")
                        prev_m3 = p.emit("vector",
                               lambda e, jj=jj, o=o: e.scalar_tensor_tensor(
                                   s_tm[:, jj, o:o + 512], lnb_tbl[:, o:o + 512], 1.0, scrA[:],
                                   ALU.bypass, ALU.add),
                               waits=[("dve", m2)], inc="dve", tag=f"y ap3 {jj}.{q4}")
                s1_done = p.cnt["dve"]
                dbg_dump("s1", poolD[:], "dve", it)

                gate()
                prev_nh = 0
                for cs in range(64):
                    view, wws, slotid, extra = wm.load("wgu1", cs, pass0)
                    c1x = cs
                    bg, bu = pb[2 * (cs % 2)], pb[2 * (cs % 2) + 1]
                    def gmm(e, view=view, bg=bg):
                        r = None
                        for fi in range(H):
                            r = e.matmul(bg[:], view[:, fi, 0:128], xn[:, fi, 3:515],
                                         start=(fi == 0), stop=(fi == H - 1))
                        return r
                    ng = p.emit("tensor", gmm,
                                waits=wws + [("dve", xn_done)] + list(bank_free[id(bg)].items()),
                                inc="pe", tag=f"f1g {c1x}")
                    def umm(e, view=view, bu=bu):
                        r = None
                        for fi in range(H):
                            r = e.matmul(bu[:], view[:, fi, 128:256],
                                         xn[:, fi, 3:515], start=(fi == 0), stop=(fi == H - 1))
                        return r
                    nu = p.emit("tensor", umm,
                                waits=wws + list(bank_free[id(bu)].items()),
                                inc="pe", tag=f"f1u {c1x}")
                    na = p.emit("scalar", lambda e, bg=bg: e.activation(sil_scr[:], bg[:], AF.Silu),
                                waits=[("pe", ng), ("dve", prev_nh)], inc="act", tag=f"f1sil {c1x}")
                    prev_nh = p.emit("vector",
                                     lambda e, c1x=c1x, bu=bu: e.scalar_tensor_tensor(
                                         hdn[:, c1x, :], sil_scr[:], 1.0, bu[:], ALU.bypass, ALU.mult),
                                     waits=[("act", na), ("pe", nu)], inc="dve", tag=f"f1hdn {c1x}")
                    bfree(bg, act=na)
                    bfree(bu, dve=prev_nh)
                    wm.release(slotid, [("pe", nu)], extra)
                hdn1_all = p.cnt["dve"]

                gate()
                g1_last = 0
                for s in range(16):
                    bank = pb[s % 2]
                    nmm = 0
                    for half in range(2):
                        view, wws, slotid, extra = wm.load("wd1", 2 * s + half, pass0)
                        def dmm(e, view=view, bank=bank, half=half):
                            r = None
                            for c in range(32):
                                r = e.matmul(bank[:], view[:, c, :], hdn[:, 32 * half + c, :],
                                             start=(half == 0 and c == 0), stop=(half == 1 and c == 31))
                            return r
                        nmm = p.emit("tensor", dmm,
                                     waits=wws + [("dve", hdn1_all)] + list(bank_free[id(bank)].items()),
                                     inc="pe", tag=f"d1 {s}.{half}")
                        wm.release(slotid, [("pe", nmm)], extra)
                    g1_last = p.emit("scalar",
                                     lambda e, s=s, bank=bank: e.activation(g_fm[:, s, :], bank[:], AF.Silu),
                                     waits=[("pe", nmm)], inc="act", tag=f"d1sil {s}")
                    bfree(bank, act=g1_last)
                dbg_dump("g1", poolE[:], "act", it)

                gate()
                resid_stage(it, "x1", "out" if mode != "l0" else "x1", g1_last, s1_done)

        try:
            _tile_loop_body()
        except _StopEmit:
            pass

        fin = p.snap()

        def final_waits(e):
            e.wait_ge(sems["da"], fin["da"])
            e.wait_ge(sems["ds"], fin["ds"])
            return e.wait_ge(sems["dg"], fin["dg"])
        p.emit("sync", final_waits, tag="final")

        p.check()

        with nc.Block() as block:
            def runner(ops):
                def f(eng):
                    for op in ops:
                        op[0](eng)
                return f
            block.tensor(runner(p.q["tensor"]))
            block.vector(runner(p.q["vector"]))
            block.scalar(runner(p.q["scalar"]))
            block.gpsimd(runner(p.q["gpsimd"]))
            block.sync(runner(p.q["sync"]))

    return nc


def kernel(**inputs):
    W, C = _host_prepare(inputs)
    x = np.ascontiguousarray(np.asarray(inputs["L0_attn_x"], np.float32))

    nc = build_program(MODE, DEBUG)
    in_maps = []
    for core in range(NCORE):
        m = {"x": x[core]}
        m.update(W)
        m.update(C)
        in_maps.append(m)
    res = run_bass_kernel_spmd(nc, in_maps, core_ids=list(range(NCORE)))
    return np.stack([np.asarray(res.results[i]["out"]) for i in range(NCORE)], axis=0)
